# version 01
# baseline (speedup 1.0000x reference)
import contextlib
import numpy as np
import ml_dtypes
import concourse.bass as bass
import concourse.mybir as mybir
from concourse.bass_utils import run_bass_kernel_spmd

F32 = mybir.dt.float32
BF16 = mybir.dt.bfloat16
ALU = mybir.AluOpType
ACT = mybir.ActivationFunctionType
AX = mybir.AxisListType
BFNP = ml_dtypes.bfloat16

NCORES = 8
B, T, D = 2, 8192, 1024
TOK, NT = 2048, 16
INNER = 4096
P5, PR = 512, 64
NH, HD = 16, 64
RC = 3.0 * 2.0 ** 22
NEG = -3.0e38
UN = 1024
NUNITS = 34

SUBLAYERS = (1, 2, 3, 4, 5, 6)
ATTN_STAGE = 6
DBG = False


class Prog:
    def __init__(self, nc):
        self.nc = nc
        self.order = ("sy", "te", "ve", "se", "ge")
        self.steps = {s: [] for s in self.order}
        self.count = dict.fromkeys(["te", "ve", "se", "ge", "sy",
                                    "dS", "dA", "dG", "cc"], 0)
        self.bufs = {}
        self.waited = {s: {} for s in self.order}
        self.pending = {}

    def _merge(self, toks, tok):
        if tok is None:
            return
        sem, val = tok
        if toks.get(sem, 0) < val:
            toks[sem] = val

    def step(self, st, fn, reads=(), writes=(), inc=None, n_inc=1, tag=""):
        inc = inc or st
        if getattr(self, "boot", False):
            reads = list(reads) + ["boot"]
        toks = {}
        for k in reads:
            b = self.bufs.get(k)
            if b:
                self._merge(toks, b["w"])
        for k in writes:
            for tok in self.pending.pop(k, ()):
                self._merge(toks, tok)
            b = self.bufs.get(k)
            if b:
                self._merge(toks, b["w"])
                for tok in b["r"]:
                    self._merge(toks, tok)
        waits = []
        for sem, val in toks.items():
            if self.waited[st].get(sem, 0) < val:
                waits.append((sem, val))
                self.waited[st][sem] = val
        unit = 16 if inc in ("dS", "dA", "dG") else 1
        if n_inc:
            self.count[inc] += unit * n_inc
            tok = (inc, self.count[inc])
            for k in reads:
                self.bufs.setdefault(k, {"w": None, "r": []})["r"].append(tok)
            for k in writes:
                self.bufs[k] = {"w": tok, "r": []}
        self.steps[st].append((waits, fn, tag))

    def barrier_tokens(self, keys):
        toks = {}
        for k in keys:
            b = self.bufs.get(k)
            if b:
                self._merge(toks, b["w"])
                for t in b["r"]:
                    self._merge(toks, t)
        return list(toks.items())

    def emit(self, sems):
        with self.nc.Block() as blk:
            for st in self.order:
                steps = self.steps[st]

                def run(eng, steps=steps):
                    for waits, fn, tag in steps:
                        for sem, val in waits:
                            eng.wait_ge(sems[sem], val)
                        fn(eng, sems)

                getattr(blk, {"te": "tensor", "ve": "vector", "se": "scalar",
                              "ge": "gpsimd", "sy": "sync"}[st])(run)


class Arena:

    def __init__(self, p, t):
        self.p = p
        self.t = t
        self.unit_keys = [set() for _ in range(NUNITS)]

    def view(self, key, u0, ncols_f32, dtype=F32, row0=0, rows=128):
        return self.view2([key], u0, ncols_f32, dtype, row0, rows)

    def view2(self, keys, u0, ncols_f32, dtype=F32, row0=0, rows=128):
        ulast = u0 + (ncols_f32 + UN - 1) // UN
        old = set()
        for u in range(u0, ulast):
            old |= self.unit_keys[u]
        toks = self.p.barrier_tokens(old - set(keys))
        if toks:
            for key in keys:
                self.p.pending.setdefault(key, []).extend(list(toks))
                self.p.bufs.pop(key, None)
        for u in range(u0, ulast):
            self.unit_keys[u] = set(keys)
        ap = self.t[row0:row0 + rows, u0 * UN:u0 * UN + ncols_f32]
        if dtype == BF16:
            ap = ap.bitcast(BF16)
        return ap


def build_program():
    nc = bass.Bass("TRN2", target_bir_lowering=False, debug=False,
                   dynamic_dma_scratch_size=12288)
    ctx = contextlib.ExitStack()

    def ein(name, shape, dtype=F32):
        return nc.dram_tensor(name, list(shape), dtype, kind="ExternalInput").ap()

    x_in = ein("x_in", [TOK, D])
    w_toP = ein("w_toP", [D, P5])
    w_U = ein("w_U", [D, P5 * 4])
    w_F = ein("w_F", [D, P5])
    w_V = ein("w_V", [1, P5 * 4])
    w_lam = ein("w_lam", [P5, PR])
    b_mat = ein("b_mat", [P5, PR])
    c_mat = ein("c_mat", [PR, P5])
    w_fromP = ein("w_fromP", [P5, D])
    w_gate = ein("w_gate", [INNER, D])
    w_up = ein("w_up", [INNER, D])
    w_down = ein("w_down", [D, INNER])
    w1_d = ein("w1", [INNER, D])
    w2_d = ein("w2", [D, INNER])
    conv_w = ein("conv_w", [INNER, 3])
    conv_b = ein("conv_b", [1, INNER])
    a_w = {nm: ein(f"W{nm}", [D, D]) for nm in "qkvo"}
    a_b = {nm: ein(f"b{nm}", [1, D]) for nm in "qkvo"}
    eye_bf_d = ein("eye_bf", [128, 128], BF16)
    eye_f32_d = ein("eye_f32", [128, 128])
    scanmask_d = ein("scanmask", [PR, 16])
    kvmask_d = ein("kvmask", [PR, 8])
    halomask_d = ein("halomask", [128, 16])
    cpool_d = ein("cpool_d", [128, 4])
    out_t = nc.dram_tensor("out", [TOK, D], F32, kind="ExternalOutput").ap()
    if DBG:
        dbg_msw = nc.dram_tensor("dbg_msw", [128, 12], F32, kind="ExternalOutput").ap()
        dbg_stats = nc.dram_tensor("dbg_stats", [128, 20, NT], F32, kind="ExternalOutput").ap()
        dbg_xq = nc.dram_tensor("dbg_xq", [128, D], BF16, kind="ExternalOutput").ap()
        dbg_actT = nc.dram_tensor("dbg_actT", [128, 1024], BF16, kind="ExternalOutput").ap()
        dbg_mid = nc.dram_tensor("dbg_mid", [128, INNER], F32, kind="ExternalOutput").ap()
        dbg_pg = nc.dram_tensor("dbg_pg", [8, 128, 512], BF16, kind="ExternalOutput").ap()
        dbg_st2 = nc.dram_tensor("dbg_st2", [128, 20, NT], F32, kind="ExternalOutput").ap()
        dbg_mq = nc.dram_tensor("dbg_mq", [128, INNER], BF16, kind="ExternalOutput").ap()

    def idram(name, shape, dtype):
        return nc.dram_tensor(name, list(shape), dtype).ap()

    p_gate = idram("p_gate", [8, 128, INNER], BF16)
    p_up = idram("p_up", [8, 128, INNER], BF16)
    p_w1 = idram("p_w1", [8, 128, INNER], BF16)
    p_down = idram("p_down", [32, 128, D], BF16)
    p_w2 = idram("p_w2", [32, 128, D], BF16)
    p_attn = {nm: idram(f"p_W{nm}", [8, 128, D], BF16) for nm in "qkvo"}
    p_weff = idram("p_weff", [8, 128, P5], BF16)
    p_wF = idram("p_wF", [8, 128, P5], BF16)
    p_fromP = idram("p_fromP", [4, 128, D], BF16)
    p_cw = idram("p_cw", [PR, D], BF16)
    cb_scan_in = idram("cb_scan_in", [PR, 2], F32)
    cb_scan_out = idram("cb_scan_out", [NCORES * PR, 2], F32)
    cb_kv_in = idram("cb_kv_in", [PR, NH * (HD + 1)], F32)
    cb_kv_out = idram("cb_kv_out", [NCORES * PR, NH * (HD + 1)], F32)
    cb_halo_in = [idram(f"cb_halo_in{i}", [128, 16], BF16) for i in range(2)]
    cb_halo_out = [idram(f"cb_halo_out{i}", [NCORES * 128, 16], BF16) for i in range(2)]

    p = Prog(nc)

    def sbuf(name, shape, dtype):
        return ctx.enter_context(nc.sbuf_tensor(name, list(shape), dtype))

    X = sbuf("X", [128, NT, D], F32)
    AR = sbuf("AR", [128, NUNITS * UN], F32)
    ar = Arena(p, AR)
    eye_b = sbuf("eye_b", [128, 128], BF16)
    eye_f = sbuf("eye_f", [128, 128], F32)
    ones_b = sbuf("ones_b", [1, 128], BF16)
    ones_rf = sbuf("ones_rf", [1, 128], F32)
    ones_cf = sbuf("ones_cf", [128, 1], F32)
    stats = sbuf("stats", [128, 20, NT], F32)
    ST = {k: i for i, k in enumerate(
        ["sum", "sumsq", "mx", "mn", "m", "var", "istd", "aa", "bb", "rs",
         "gsc", "usc", "mmx", "mmn", "beta", "dsc", "t0", "t1", "t2", "t3"])}
    msw = sbuf("msw", [128, 12], F32)
    MS = {k: i for i, k in enumerate(
        ["gate", "up", "down", "w1", "w2", "gate_r", "up_r", "down_r",
         "w1_r", "w2_r", "tmp", "tmp2"])}
    colscr = sbuf("colscr", [128, 40], F32)
    taps = sbuf("taps", [128, 32, 4], F32)
    masks_s = sbuf("masks_s", [PR, 16], F32)
    masks_kv = sbuf("masks_kv", [PR, 8], F32)
    masks_h = sbuf("masks_h", [128, 16], F32)
    wlam_s = sbuf("wlam_s", [128, 4, PR], BF16)
    bmat_s = sbuf("bmat_s", [128, 4, PR], BF16)
    cmT_s = sbuf("cmT_s", [128, 4, PR], BF16)
    carry = sbuf("carry", [PR, 1], F32)
    scAB = sbuf("scAB", [PR, 4], F32)
    gath = sbuf("gath", [PR, 16], F32)
    dn_s = sbuf("dn_s", [128, NH], F32)
    rd_s = sbuf("rd_s", [128, NH], F32)
    halo_g = sbuf("halo_g", [128, 8, 16], BF16)
    halo_col = sbuf("halo_col", [128, 4, 8], BF16)
    xq_tm = [sbuf(f"xq_tm{i}", [128, D], BF16) for i in range(2)]
    pubs = sbuf("pubs", [PR, 2], F32)

    cpool = sbuf("cpool", [128, 4], F32)
    nc.const_aps.register(F32, RC, cpool[:, 0:1])
    nc.const_aps.register(F32, 1e-5, cpool[:, 1:2])
    nc.const_aps.register(F32, 0.0, cpool[:, 2:3])
    nc.const_aps.register(F32, 1.0, cpool[:, 3:4])

    pb = [ctx.enter_context(nc.psum_tensor(f"pb{i}", [128, 512], F32)) for i in range(8)]

    def pbf(i):
        return pb[i][:].bitcast(BF16)

    semnames = ["te", "ve", "se", "ge", "sy", "dS", "dA", "dG", "cc"]
    sems = {n: ctx.enter_context(nc.semaphore(f"sem_{n}")) for n in semnames}

    cur = {}

    def dma(st_, dst, src, reads, writes, inc, tag="", slow=False):
        p.step(st_, lambda e, s, d=dst, sr=src: e.dma_start(
            out=d, in_=sr, allow_slow_non_contiguous=slow)
               .then_inc(s[inc], 16), reads=reads, writes=writes, inc=inc, tag=tag)

    def ve(fn, reads, writes, tag=""):
        p.step("ve", lambda e, s: fn(e).then_inc(s["ve"], 1), reads=reads,
               writes=writes, tag=tag)

    def ge(fn, reads, writes, tag=""):
        p.step("ge", lambda e, s: fn(e).then_inc(s["ge"], 1), reads=reads,
               writes=writes, tag=tag)

    def se(fn, reads, writes, tag=""):
        p.step("se", lambda e, s: fn(e).then_inc(s["se"], 1), reads=reads,
               writes=writes, tag=tag)

    def col(name, t):
        return stats[:, ST[name], t:t + 1]

    def cols(name, lo, hi):
        return stats[:, ST[name], lo:hi]

    def mcol(name):
        return msw[:, MS[name]:MS[name] + 1]

    def mm_group(bank, pairs, reads, tag="", first=True, last=True):
        def fn(e, s):
            n = len(pairs)
            for i, (o, l, r) in enumerate(pairs):
                inst = e.matmul(o, l, r, start=(first and i == 0),
                                stop=(last and i == n - 1))
                if i == n - 1:
                    inst.then_inc(s["te"], 1)
        p.step("te", fn, reads=reads, writes=[f"pb{bank}"], tag=tag)

    def transpose_pack_f(block_fn, src_keys, dst_fn, dst_keys, nblocks, banks,
                         bf=True, tag=""):
        per = 8 if bf else 4
        idt, idk = (eye_b, "eye_b") if bf else (eye_f, "eye_f")
        view = pbf if bf else (lambda b: pb[b][:])
        for pk in range((nblocks + per - 1) // per):
            bk_ = banks[pk % len(banks)]
            blocks = list(range(pk * per, min(pk * per + per, nblocks)))

            def tr_fn(e, s, blocks=blocks, bk_=bk_):
                for j, c in enumerate(blocks):
                    e.transpose(view(bk_)[:, j * 128:(j + 1) * 128],
                                block_fn(c), idt[:]).then_inc(s["te"], 1)
            p.step("te", tr_fn, reads=list(src_keys) + [idk],
                   writes=[f"pb{bk_}"], n_inc=len(blocks), tag=tag)
            nb = len(blocks)
            se(lambda e, pk=pk, nb=nb, bk_=bk_: e.copy(
                dst_fn(pk, nb), view(bk_)[:, :nb * 128]),
               [f"pb{bk_}"], [dst_keys(pk)], tag=tag)

    def transpose_pack(src_ap, src_keys, dst_ap, dst_key, nblocks, banks,
                       bf=True, tag=""):
        per = 8 if bf else 4
        transpose_pack_f(lambda c: src_ap[:, c * 128:(c + 1) * 128], src_keys,
                         lambda pk, nb: dst_ap[:, pk * per * 128:
                                               pk * per * 128 + nb * 128],
                         lambda pk: dst_key, nblocks, banks, bf=bf, tag=tag)

    dma("se", eye_b[:], eye_bf_d, [], ["eye_b"], "dA")
    dma("se", eye_f[:], eye_f32_d, [], ["eye_f"], "dA")
    dma("se", masks_s[:], scanmask_d, [], ["masks_s"], "dA")
    dma("se", masks_kv[:], kvmask_d, [], ["masks_kv"], "dA")
    dma("se", masks_h[:], halomask_d, [], ["masks_h"], "dA")
    ve(lambda e: e.memset(ones_b[:], 1.0), [], ["ones_b"])
    ve(lambda e: e.memset(ones_rf[:], 1.0), [], ["ones_rf"])
    ve(lambda e: e.memset(ones_cf[:], 1.0), [], ["ones_cf"])
    dma("se", cpool[:], cpool_d, [], ["boot"], "dA")
    p.boot = True
    dma("se", X[:], x_in.rearrange("(t p) d -> p t d", p=128),
        [], [f"x{t}" for t in range(NT)], "dA")

    def prep_quant_resident(w_dram, p_dram, mkey, rows, colsn, tag):
        nt_ = rows // 128
        wres = ar.view(f"wres_{tag}", 0, nt_ * colsn)
        fscr = ar.view(f"fscr_{tag}", 32, 1024)
        bs2 = ar.view(f"bs2_{tag}", 33, 1024)
        bscr = bs2[:, 0:512].bitcast(BF16)
        stage = bs2[:, 512:1024].bitcast(BF16)
        acc_k = f"pacc_{tag}"
        for i in range(nt_):
            dma("sy", wres[:, i * colsn:(i + 1) * colsn],
                w_dram[i * 128:(i + 1) * 128, :],
                [f"wres_{tag}"] if i == 0 else [], [f"wres_{tag}"], "dS",
                tag=f"p{tag}")
            ve(lambda e, i=i: e.reduce_sum(colscr[:, i:i + 1],
                                           wres[:, i * colsn:(i + 1) * colsn],
                                           AX.X, apply_absolute_value=True),
               [f"wres_{tag}"], [acc_k])
        ve(lambda e: e.reduce_sum(colscr[:, 32:33], colscr[:, 0:nt_], AX.X),
           [acc_k], [acc_k])
        mm_group(7, [(pb[7][0:1, 0:1], colscr[:, 32:33], ones_cf[:, :])],
                 [acc_k, "ones_cf"], tag=f"ps{tag}")
        ve(lambda e: e.tensor_scalar(colscr[0:1, 33:34], pb[7][0:1, 0:1],
                                     1.0 / (rows * colsn), 1e-5, ALU.mult, ALU.max),
           ["pb7"], ["ptiny"])
        mm_group(7, [(pb[7][:, 0:1], ones_rf[:, :], colscr[0:1, 33:34])],
                 ["ptiny", "ones_rf"], tag=f"pb{tag}")
        ve(lambda e: e.tensor_copy(mcol(mkey), pb[7][:, 0:1]), ["pb7"], [f"ms_{mkey}"])
        ve(lambda e: e.reciprocal(mcol(mkey + "_r"), mcol(mkey)),
           [f"ms_{mkey}"], [f"ms_{mkey}_r"])
        nsl = (colsn + 1023) // 1024
        for i in range(nt_):
            for sl in range(nsl):
                c0 = sl * 1024
                cw = min(1024, colsn - c0)
                src = wres[:, i * colsn + c0: i * colsn + c0 + cw]
                se(lambda e, src=src, cw=cw: e.activation(
                    fscr[:, :cw], src, ACT.Identity, bias=RC,
                    scale=mcol(mkey + "_r")),
                   [f"wres_{tag}", f"ms_{mkey}_r"], [f"fscr_{tag}"])
                ve(lambda e, cw=cw: e.tensor_scalar(fscr[:, :cw], fscr[:, :cw],
                                                    -RC, None, ALU.add),
                   [f"fscr_{tag}"], [f"fscr_{tag}"])
                ve(lambda e, cw=cw: e.tensor_scalar(
                    bscr[:, :cw], fscr[:, :cw], 1.0, -1.0, ALU.min, ALU.max),
                   [f"fscr_{tag}"], [f"bscr_{tag}"])
                nb = cw // 128
                transpose_pack(bscr, [f"bscr_{tag}"], stage, f"stage_{tag}",
                               nb, [6, 7], tag=f"pt{tag}")
                dc0 = c0 // 128

                def st_fn(e, s, dc0=dc0, nb=nb, i=i):
                    e.dma_start(
                        out=p_dram[dc0:dc0 + nb, :, i * 128:(i + 1) * 128]
                        .transpose([1, 0, 2]),
                        in_=stage[:, :nb * 128].rearrange(
                            "p (c r) -> p c r", c=nb)).then_inc(s["dS"], 16)
                p.step("sy", st_fn, reads=[f"stage_{tag}"], writes=[f"pd_{tag}"],
                       inc="dS", tag=f"po{tag}")

    def prep_attn_w(w_dram, p_dram, tag, u0=32):
        braw = ar.view(f"braw_{tag}", u0, 1024, BF16)
        stg = ar.view(f"stg_{tag}", u0 + 1, 512, BF16)
        for i in range(8):
            dma("ge", braw[:, 0:1024], w_dram[i * 128:(i + 1) * 128, :],
                [f"braw_{tag}"], [f"braw_{tag}"], "dG", tag=f"pa{tag}")
            transpose_pack(braw, [f"braw_{tag}"], stg, f"stg_{tag}", 8, [6, 7],
                           tag=f"pa{tag}")

            def st_fn(e, s, i=i):
                e.dma_start(out=p_dram[:, :, i * 128:(i + 1) * 128]
                            .transpose([1, 0, 2]),
                            in_=stg[:, 0:1024].rearrange("p (c r) -> p c r", c=8)) \
                    .then_inc(s["dS"], 16)
            p.step("sy", st_fn, reads=[f"stg_{tag}"], writes=[f"pd_{tag}"], inc="dS")

    def prep_cast(w_dram, p_dram, nchunks, colsn, tag, u0=32):
        braw = ar.view(f"braw_{tag}", u0, 1024, BF16)
        for i in range(nchunks):
            dma("ge", braw[:, 0:colsn], w_dram[i * 128:(i + 1) * 128, :],
                [f"braw_{tag}"], [f"braw_{tag}"], "dG")
            dma("sy", p_dram[i, :, :], braw[:, 0:colsn], [f"braw_{tag}"],
                [f"pd_{tag}"], "dS")

    def prep_ssm():
        wus = ar.view("wus", 0, 16 * UN)
        wvb = ar.view("wvb", 16, 2048)
        escr = ar.view("escr", 18, 1024)
        wtp = ar.view("wtp", 19, 512)
        dma("se", wvb[0:1, :], w_V, ["wvb"], ["wvb_row"], "dA")
        for q in range(4):
            mm_group(5, [(pb[5][:, :], ones_rf[:, :], wvb[0:1, q * 512:(q + 1) * 512])],
                     ["wvb_row", "ones_rf"], tag="wvb")
            ve(lambda e, q=q: e.tensor_copy(wvb[:, q * 512:(q + 1) * 512], pb[5][:, :]),
               ["pb5"], ["wvb"])
        for dc in range(8):
            dma("sy", wus[:, dc * 2048:(dc + 1) * 2048],
                w_U[dc * 128:(dc + 1) * 128, :],
                ["wus"] if dc == 0 else [], ["wus"], "dS")
        for dc in range(8):
            wuv = wus[:, dc * 2048:(dc + 1) * 2048].rearrange("p (q a) -> p q a", a=4)
            wvv = wvb[:].rearrange("p (q a) -> p q a", a=4)
            dma("se", wtp[:, :], w_toP[dc * 128:(dc + 1) * 128, :],
                ["wtp"], ["wtp"], "dA")
            ve(lambda e, wuv=wuv, wvv=wvv: e.tensor_mul(
                escr[:, 0:512], wuv[:, :, 0], wvv[:, :, 0]),
               ["wus", "wvb"], ["escr0"])
            for a in (1, 2, 3):
                ve(lambda e, a=a, wuv=wuv, wvv=wvv: e.tensor_mul(
                    escr[:, 512:1024], wuv[:, :, a], wvv[:, :, a]),
                   ["wus", "wvb"], ["escr1"])
                ve(lambda e: e.tensor_add(escr[:, 0:512], escr[:, 0:512],
                                          escr[:, 512:1024]),
                   ["escr0", "escr1"], ["escr0"])
            ve(lambda e: e.tensor_add(escr[:, 512:1024].bitcast(BF16)[:, 0:512],
                                      escr[:, 0:512], wtp[:, :]),
               ["escr0", "wtp"], ["escr1"])
            dma("sy", p_weff[dc, :, :], escr[:, 512:1024].bitcast(BF16)[:, 0:512],
                ["escr1"], ["pd_weff"], "dS")
        prep_cast(w_F, p_wF, 8, P5, "wF", u0=17)
        prep_cast(w_fromP, p_fromP, 4, D, "fromP", u0=17)
        dma("ge", wlam_s[:], w_lam.rearrange("(f p) o -> p f o", p=128),
            [], ["wlam_s"], "dG")
        dma("ge", bmat_s[:], b_mat.rearrange("(f p) o -> p f o", p=128),
            [], ["bmat_s"], "dG")
        craw = ar.view("craw", 19, 256, BF16)
        dma("ge", craw[0:64, 0:512], c_mat, ["craw"], ["craw"], "dG")

        def trc(e, s):
            for j in range(4):
                e.transpose(pbf(5)[:, j * 64:(j + 1) * 64],
                            craw[0:64, j * 128:(j + 1) * 128],
                            eye_b[0:64, 0:64]).then_inc(s["te"], 1)
        p.step("te", trc, reads=["craw", "eye_b"], writes=["pb5"], n_inc=4)
        ve(lambda e: e.tensor_copy(cmT_s[:].rearrange("p f o -> p (f o)"),
                                   pbf(5)[:, 0:256]), ["pb5"], ["cmT_s"])
        cwc = ar.view("cwc", 20, 512, BF16)
        for pc in range(4):
            dma("sy", cwc[:, 0:1024], p_fromP[pc, :, :], ["cwc", "pd_fromP"], ["cwc"], "dS")
            for h in range(2):
                mm_group(3 + h, [(pb[3 + h][0:64, :], cmT_s[:, pc, :],
                                  cwc[:, h * 512:(h + 1) * 512])],
                         ["cwc", "cmT_s"], first=(pc == 0), last=(pc == 3),
                         tag="cw")
        cwo = ar.view("cwo", 21, 512, BF16)
        for h in range(2):
            ve(lambda e, h=h: e.tensor_copy(cwo[0:64, h * 512:(h + 1) * 512],
                                            pb[3 + h][0:64, :]),
               [f"pb{3 + h}"], ["cwo"])
        dma("sy", p_cw, cwo[0:64, 0:1024], ["cwo"], ["pd_cw"], "dS")

    def prep_taps():
        dma("se", taps[:, :, 0:3],
            conv_w.rearrange("(m p) k -> p m k", p=128), [], ["taps_raw"], "dA")
        dma("se", taps[:, :, 3:4], conv_b.rearrange("o (m p) -> p m o", p=128),
            [], ["taps_b"], "dA", slow=True)
        for k in range(3):
            ve(lambda e, k=k: e.tensor_scalar(taps[:, :, k], taps[:, :, k],
                                              mcol("w1"), None, ALU.mult),
               ["taps_raw", "ms_w1"], ["taps_raw"])

    def stats_tile(t, quant):
        xk = f"x{t}"
        ve(lambda e: e.reduce_sum(col("sum", t), X[:, t, :], AX.X), [xk], ["st_sum"])
        se(lambda e: e.activation(cur["sscr"][:, 0:1024], X[:, t, :], ACT.Square,
                                  accum_out=col("sumsq", t)),
           [xk], ["sscr", "st_sumsq"])
        if quant:
            ve(lambda e: e.reduce_max(col("mx", t), X[:, t, :], AX.X), [xk], ["st_mx"])
            ve(lambda e: e.tensor_reduce(col("mn", t), X[:, t, :], AX.X, ALU.min),
               [xk], ["st_mn"])

    def tiny_ln(quant):
        A = lambda n: stats[:, ST[n], 0:NT]
        ve(lambda e: e.tensor_scalar(A("m"), A("sum"), 1.0 / D, None, ALU.mult),
           ["st_sum"], ["st_m"])
        ve(lambda e: e.tensor_scalar(A("var"), A("sumsq"), 1.0 / D, None, ALU.mult),
           ["st_sumsq"], ["st_var"])
        ve(lambda e: e.tensor_mul(A("t0"), A("m"), A("m")), ["st_m"], ["st_t0"])
        ve(lambda e: e.tensor_sub(A("var"), A("var"), A("t0")),
           ["st_var", "st_t0"], ["st_var"])
        se(lambda e: e.activation(A("t1"), A("var"), ACT.Sqrt, bias=1e-5, scale=1.0),
           ["st_var"], ["st_t1"])
        ve(lambda e: e.reciprocal(A("istd"), A("t1")), ["st_t1"], ["st_istd"])
        if not quant:
            ve(lambda e: e.tensor_copy(A("aa"), A("istd")), ["st_istd"], ["st_aa"])
            ve(lambda e: e.scalar_tensor_tensor(A("bb"), A("m"), -1.0, A("istd"),
                                                ALU.mult, ALU.mult),
               ["st_m", "st_istd"], ["st_bb"])
            return
        ve(lambda e: e.tensor_sub(A("t0"), A("mx"), A("m")),
           ["st_mx", "st_m"], ["st_t0"])
        ve(lambda e: e.tensor_sub(A("t2"), A("m"), A("mn")),
           ["st_mn", "st_m"], ["st_t2"])
        ve(lambda e: e.tensor_max(A("t0"), A("t0"), A("t2")),
           ["st_t0", "st_t2"], ["st_t0"])
        ve(lambda e: e.tensor_mul(A("t0"), A("t0"), A("istd")),
           ["st_t0", "st_istd"], ["st_t0"])
        ve(lambda e: e.tensor_scalar(A("t0"), A("t0"), 1e-5, None, ALU.max),
           ["st_t0"], ["st_t0"])
        ve(lambda e: e.tensor_scalar(A("rs"), A("t0"), 1.0 / 127.0, None, ALU.mult),
           ["st_t0"], ["st_rs"])
        ve(lambda e: e.reciprocal(A("t1"), A("t0")), ["st_t0"], ["st_t1"])
        ve(lambda e: e.scalar_tensor_tensor(A("aa"), A("istd"), 127.0, A("t1"),
                                            ALU.mult, ALU.mult),
           ["st_istd", "st_t1"], ["st_aa"])
        ve(lambda e: e.scalar_tensor_tensor(A("bb"), A("m"), -1.0, A("aa"),
                                            ALU.mult, ALU.mult),
           ["st_m", "st_aa"], ["st_bb"])

    def quant_tile(t, mode):
        qb = t % 2
        if mode == "ln":
            ve(lambda e: e.tensor_scalar(xq_tm[qb][:], X[:, t, :], col("aa", t),
                                         col("bb", t), ALU.mult, ALU.add),
               [f"x{t}", "st_aa", "st_bb"], [f"xq{qb}"])
            return
        nq = len(cur["qscr"])
        q = cur["qscr"][qb % nq]
        qk = f"qscr{qb % nq}"
        ve(lambda e: e.tensor_scalar(q, X[:, t, :], col("aa", t), col("bb", t),
                                     ALU.mult, ALU.add),
           [f"x{t}", "st_aa", "st_bb"], [qk])
        if mode == "int":
            ve(lambda e: e.tensor_scalar(xq_tm[qb][:], q, RC, -RC, ALU.add, ALU.add),
               [qk], [f"xq{qb}"])
        else:
            ve(lambda e: e.tensor_scalar(q, q, RC, -RC, ALU.add, ALU.add), [qk], [qk])
            ve(lambda e: e.tensor_scalar(xq_tm[qb][:], q, col("rs", t), None,
                                         ALU.mult),
               [qk, "st_rs"], [f"xq{qb}"])

    def ffn0():
        mid = [ar.view(f"mid{i}", 4 * i, 4096) for i in range(4)]
        wA = [ar.view(f"wA{i}", 16 + 2 * i, 2048, BF16) for i in range(2)]
        wB = [ar.view(f"wB{i}", 20 + 2 * i, 2048, BF16) for i in range(2)]
        actTb = ar.view("actTg", 24, 2048, BF16)
        mqT = ar.view("mqT", 26, 2048, BF16)
        cur["qscr"] = [ar.view(f"qscr{i}", 28 + i, 1024) for i in range(2)]
        cur["sscr"] = ar.view("sscr", 30, 1024)
        wCu = ar.view2(["wC0", "wC1"], 31, 512, BF16)
        wC = [wCu[:, 0:512], wCu[:, 512:1024]]
        mq_tm = ar.view("mq_tm", 32, 2048, BF16)
        wAv = [w.rearrange("p (c n) -> p c n", c=8) for w in wA]
        wBv = [w.rearrange("p (c n) -> p c n", c=8) for w in wB]

        for t in range(NT):
            stats_tile(t, True)
        tiny_ln(True)
        ve(lambda e: e.tensor_scalar(cols("gsc", 0, NT), cols("rs", 0, NT),
                                     mcol("gate"), None, ALU.mult),
           ["st_rs", "ms_gate"], ["st_gsc"])
        ve(lambda e: e.tensor_scalar(cols("usc", 0, NT), cols("rs", 0, NT),
                                     mcol("up"), None, ALU.mult),
           ["st_rs", "ms_up"], ["st_usc"])
        for g in range(4):
            tiles = list(range(4 * g, 4 * g + 4))
            for t in tiles:
                quant_tile(t, "int")
                qb, j = t % 2, t % 4
                if DBG and t == 0:
                    dma("se", dbg_xq, xq_tm[qb][:], [f"xq{qb}"], ["dbg4"], "dA")
                transpose_pack(xq_tm[qb][:], [f"xq{qb}"],
                               actTb[:, j * 1024:(j + 1) * 1024], f"actT{j}",
                               8, [6, 7], tag="xqT")
                if DBG and t == 0:
                    dma("se", dbg_actT, actTb[:, 0:1024], ["actT0"], ["dbg5"], "dA")
            for n in range(8):
                par = n % 2
                dma("sy", wAv[par],
                    p_gate[:, :, n * 512:(n + 1) * 512].transpose([1, 0, 2]),
                    [f"wA{par}", "pd_g"], [f"wA{par}"], "dS", tag="wg")
                dma("sy", wBv[par],
                    p_up[:, :, n * 512:(n + 1) * 512].transpose([1, 0, 2]),
                    [f"wB{par}", "pd_u"], [f"wB{par}"], "dS", tag="wu")
                for t in tiles:
                    j = t % 4
                    aT = actTb[:, j * 1024:(j + 1) * 1024]
                    bg, bu = (0, 1) if t % 2 == 0 else (2, 3)
                    mm_group(bg, [(pb[bg][:, :], aT[:, dc * 128:(dc + 1) * 128],
                                   wAv[par][:, dc, :]) for dc in range(8)],
                             [f"actT{j}", f"wA{par}"], tag="mmg")
                    mm_group(bu, [(pb[bu][:, :], aT[:, dc * 128:(dc + 1) * 128],
                                   wBv[par][:, dc, :]) for dc in range(8)],
                             [f"actT{j}", f"wB{par}"], tag="mmu")
                    se(lambda e, bg=bg, t=t: e.activation(
                        cur["sscr"][:, 0:512], pb[bg][:, :], ACT.Silu, bias=0.0,
                        scale=col("gsc", t)),
                       [f"pb{bg}", "st_gsc"], ["sscr"], tag="silu")
                    ve(lambda e, j=j, n=n, bu=bu:
                       e.tensor_mul(mid[j][:, n * 512:(n + 1) * 512],
                                    cur["sscr"][:, 0:512], pb[bu][:, :]),
                       ["sscr", f"pb{bu}"], [f"mid{j}"], tag="midmul")
            for t in tiles:
                j = t % 4
                if DBG and t == 0:
                    dma("se", dbg_mid, mid[0][:], ["mid0"], ["dbg6"], "dA")
                ve(lambda e, j=j, t=t: e.tensor_reduce(col("mmx", t), mid[j][:],
                                                       AX.X, ALU.max,
                                                       apply_absolute_value=True),
                   [f"mid{j}"], ["st_mmx"])
            def tiny_group(lo, hi):
                A2 = lambda n: stats[:, ST[n], lo:hi]
                ve(lambda e: e.tensor_mul(A2("t0"), A2("mmx"), A2("usc")),
                   ["st_mmx", "st_usc"], ["st_t0"])
                ve(lambda e: e.tensor_scalar(A2("t0"), A2("t0"), 1e-5, None, ALU.max),
                   ["st_t0"], ["st_t0"])
                ve(lambda e: e.reciprocal(A2("t1"), A2("t0")), ["st_t0"], ["st_t1"])
                ve(lambda e: e.scalar_tensor_tensor(A2("beta"), A2("usc"), 127.0,
                                                    A2("t1"), ALU.mult, ALU.mult),
                   ["st_usc", "st_t1"], ["st_beta"])
                ve(lambda e: e.tensor_scalar(A2("dsc"), A2("t0"), mcol("down"),
                                             None, ALU.mult),
                   ["st_t0", "ms_down"], ["st_dsc"])
                ve(lambda e: e.tensor_scalar(A2("dsc"), A2("dsc"), 0.5 / 127.0,
                                             None, ALU.mult),
                   ["st_dsc"], ["st_dsc"])
            tiny_group(4 * g, 4 * g + 4)
            if DBG and g == 0:
                dma("se", dbg_st2, stats[:], ["st_beta", "st_dsc", "st_t1", "st_t0",
                                              "st_mmx", "st_usc"], ["dbg7"], "dA")
            for t in tiles:
                j = t % 4
                se(lambda e, j=j, t=t: e.activation(mid[j][:], mid[j][:],
                                                    ACT.Identity, bias=RC,
                                                    scale=col("beta", t)),
                   [f"mid{j}", "st_beta"], [f"mid{j}"], tag="mq1")
                ve(lambda e, j=j: e.tensor_scalar(mq_tm[:], mid[j][:], -RC,
                                                  None, ALU.add),
                   [f"mid{j}"], ["mq_tm"], tag="mq2")
                if DBG and t == 0:
                    dma("se", dbg_mq, mq_tm[:], ["mq_tm"], ["dbg8"], "dA")
                transpose_pack(mq_tm, ["mq_tm"], mqT, "mqT", 32, [6, 7], tag="mqT")
                for h in range(2):
                    bd = 4 + h
                    for k in range(32):
                        par = k % 2
                        dma("sy", wC[par], p_down[k, :, h * 512:(h + 1) * 512],
                            [f"wC{par}", "pd_d"], [f"wC{par}"], "dS", tag="wd")
                        mm_group(bd, [(pb[bd][:, :], mqT[:, k * 128:(k + 1) * 128],
                                       wC[par])],
                                 ["mqT", f"wC{par}"], first=(k == 0),
                                 last=(k == 31), tag="mmd")
                    ve(lambda e, t=t, h=h, bd=bd: e.scalar_tensor_tensor(
                        X[:, t, h * 512:(h + 1) * 512], pb[bd][:, :],
                        col("dsc", t), X[:, t, h * 512:(h + 1) * 512],
                        ALU.mult, ALU.add),
                       [f"pb{bd}", "st_dsc", f"x{t}"], [f"x{t}"], tag="xup")

    def ssm():
        xlnT = ar.view("xlnT", 0, 2048, BF16)
        tfm = ar.view("tfm", 4, 4096, BF16)
        fgfm = ar.view("fgfm", 8, 4096, BF16)
        lam = ar.view("lam", 12, 2048, F32, rows=PR)
        uu = ar.view("uu", 14, 2048, F32, rows=PR)
        As = ar.view("As", 16, 2048, F32, rows=PR)
        Bs = ar.view("Bs", 18, 2048, F32, rows=PR)
        sbfv = ar.view("sbf", 20, 1024, BF16, rows=PR)
        wE = ar.view("wE", 21, 2048, BF16)
        wFf = ar.view("wFf", 23, 2048, BF16)
        wPf = ar.view("wPf", 25, 2048, BF16)
        cur["sscr"] = ar.view("sscr", 27, 1024)
        cw_s = ar.view("cw_s", 28, 512, BF16, rows=PR)
        xlnTv = xlnT.rearrange("p (c n) -> p c n", c=8)
        tfmv = tfm.rearrange("p (c n) -> p c n", c=4)
        fgfmv = fgfm.rearrange("p (c n) -> p c n", c=4)
        wEv = wE.rearrange("p (c n) -> p c n", c=8)
        wFv = wFf.rearrange("p (c n) -> p c n", c=8)
        wPv = wPf.rearrange("p (c n) -> p c n", c=4)
        dma("sy", cw_s[:, 0:1024], p_cw, ["cw_s", "pd_cw"], ["cw_s"], "dS")

        dma("sy", wEv, p_weff.transpose([1, 0, 2]), ["wE", "pd_weff"], ["wE"], "dS")
        dma("sy", wFv, p_wF.transpose([1, 0, 2]), ["wFf", "pd_wF"], ["wFf"], "dS")
        dma("sy", wPv, p_fromP.transpose([1, 0, 2]), ["wPf", "pd_fromP"], ["wPf"], "dS")
        for t in range(NT):
            stats_tile(t, False)
        tiny_ln(False)
        for g in range(4):
            tiles = list(range(4 * g, 4 * g + 4))
            for t in tiles:
                quant_tile(t, "ln")
                qb = t % 2
                bk_ = 6 + (t % 2)

                def tr_fn(e, s, qb=qb, bk_=bk_):
                    for c in range(8):
                        e.transpose(pbf(bk_)[:, c * 128:(c + 1) * 128],
                                    xq_tm[qb][:, c * 128:(c + 1) * 128],
                                    eye_b[:]).then_inc(s["te"], 1)
                p.step("te", tr_fn, reads=[f"xq{qb}", "eye_b"],
                       writes=[f"pb{bk_}"], n_inc=8, tag="xlnT")
                se(lambda e, bk_=bk_, t=t: e.copy(
                    xlnTv[:, :, (t % 4) * 128:(t % 4) * 128 + 128],
                    pbf(bk_)[:, 0:1024].rearrange("p (c n) -> p c n", c=8)),
                   [f"pb{bk_}"], ["xlnTg"], tag="xlnTe")
            gc = slice(g * 512, (g + 1) * 512)
            for pc in range(4):
                bt, bf_ = pc % 2, 2 + pc % 2
                mm_group(bt, [(pb[bt][:, :], wEv[:, dc, pc * 128:(pc + 1) * 128],
                               xlnTv[:, dc, :]) for dc in range(8)],
                         ["wE", "xlnTg"], tag="mmt")
                se(lambda e, pc=pc, bt=bt, gc=gc: e.copy(
                    tfmv[:, pc, gc], pb[bt][:, :]),
                   [f"pb{bt}"], ["tfm"], tag="tfme")
                mm_group(bf_, [(pb[bf_][:, :], wFv[:, dc, pc * 128:(pc + 1) * 128],
                                xlnTv[:, dc, :]) for dc in range(8)],
                         ["wFf", "xlnTg"], tag="mmf")
                se(lambda e, pc=pc, bf_=bf_, gc=gc: e.activation(
                    fgfmv[:, pc, gc], pb[bf_][:, :], ACT.Sigmoid),
                   [f"pb{bf_}"], ["fgfm"], tag="fge")
            mm_group(4, [(pb[4][0:64, :], wlam_s[:, pc, :], fgfmv[:, pc, gc])
                         for pc in range(4)], ["wlam_s", "fgfm"], tag="mml")
            se(lambda e, gc=gc: e.activation(lam[:, gc], pb[4][0:64, :],
                                             ACT.Sigmoid),
               ["pb4"], ["lam"], tag="lame")
            mm_group(5, [(pb[5][0:64, :], bmat_s[:, pc, :], tfmv[:, pc, gc])
                         for pc in range(4)], ["bmat_s", "tfm"], tag="mmu2")
            se(lambda e, gc=gc: e.copy(uu[:, gc], pb[5][0:64, :]),
               ["pb5"], ["uu"], tag="ue")
        ve(lambda e: e.tensor_tensor_scan(As[:], lam[:], lam[:], 1.0,
                                          ALU.mult, ALU.bypass),
           ["lam"], ["As"], tag="scanA")
        ve(lambda e: e.tensor_tensor_scan(Bs[:], lam[:], uu[:], 0.0,
                                          ALU.mult, ALU.add),
           ["lam", "uu"], ["Bs"], tag="scanB")
        ve(lambda e: e.tensor_copy(pubs[:, 0:1], As[:, 2047:2048]),
           ["As"], ["pubs"])
        ve(lambda e: e.tensor_copy(pubs[:, 1:2], Bs[:, 2047:2048]),
           ["pubs", "Bs"], ["pubs"])
        dma("se", cb_scan_in, pubs[:], ["pubs"], ["cbsi"], "dA")
        p.step("ge", lambda e, s: e.collective_compute(
            "AllGather", ALU.bypass, replica_groups=[list(range(NCORES))],
            ins=[cb_scan_in.opt()], outs=[cb_scan_out.opt()]).then_inc(s["cc"], 1),
            reads=["cbsi"], writes=["cbso"], inc="cc", tag="ccscan")
        dma("se", gath[:].rearrange("p (j k) -> p j k", j=8),
            cb_scan_out.rearrange("(j p) k -> p j k", p=PR),
            ["cbso"], ["gath"], "dA")
        ve(lambda e: e.memset(carry[:], 0.0), ["carry"], ["carry"])
        for j in range(NCORES):
            ve(lambda e, j=j: e.tensor_scalar(scAB[:, 0:1], gath[:, 2 * j:2 * j + 1],
                                              masks_s[:, j:j + 1],
                                              masks_s[:, 8 + j:8 + j + 1],
                                              ALU.mult, ALU.add),
               ["gath", "masks_s"], ["scA"])
            ve(lambda e, j=j: e.tensor_scalar(scAB[:, 1:2],
                                              gath[:, 2 * j + 1:2 * j + 2],
                                              masks_s[:, j:j + 1], None, ALU.mult),
               ["gath", "masks_s"], ["scB"])
            ve(lambda e: e.scalar_tensor_tensor(carry[:], carry[:], scAB[:, 0:1],
                                                scAB[:, 1:2], ALU.mult, ALU.add),
               ["carry", "scA", "scB"], ["carry"])
        ve(lambda e: e.scalar_tensor_tensor(As[:], As[:], carry[:, 0:1], Bs[:],
                                            ALU.mult, ALU.add),
           ["As", "Bs", "carry"], ["As"], tag="sfin")
        ve(lambda e: e.tensor_copy(sbfv[:], As[:]), ["As"], ["sbf"], tag="scast")
        for t in range(NT):
            tc = slice(t * 128, (t + 1) * 128)
            for h in range(2):
                bh = h
                pairs = [(pb[bh][:, :], tfmv[:, pc, tc],
                          wPv[:, pc, h * 512:(h + 1) * 512]) for pc in range(4)]
                pairs.append((pb[bh][:, :], sbfv[:, tc], cw_s[0:64, h * 512:(h + 1) * 512]))
                mm_group(bh, pairs, ["tfm", "sbf", "wPf", "cw_s"], tag="mmh")
                ve(lambda e, t=t, h=h, bh=bh: e.scalar_tensor_tensor(
                    X[:, t, h * 512:(h + 1) * 512], pb[bh][:, :], 1.0,
                    X[:, t, h * 512:(h + 1) * 512], ALU.mult, ALU.add),
                   [f"pb{bh}", f"x{t}"], [f"x{t}"], tag="xh")

    def l1ffn(call):
        xqdT = ar.view("xqdT", 0, 8200, BF16)
        xqdTv = xqdT[:, 0:8 * 2050].rearrange("p (c n) -> p c n", c=8)
        h2g = ar.view("h2g", 9, 16384)
        h2gv = h2g.rearrange("p (m n) -> p m n", m=32)
        h2tm = ar.view("h2tm", 25, 4096)
        h2qTu = ar.view2(["h2qT0", "h2qT1", "h2qT2", "h2qT3"], 29, 2048, BF16)
        cur["qscr"] = [ar.view("qscr0", 31, 1024)]
        cur["sscr"] = ar.view("sscr", 32, 1024)
        w1cb = ar.view2(["w1c0", "w1c1"], 33, 1024, BF16)
        w1c = [w1cb[:, 0:1024], w1cb[:, 1024:2048]]
        w1cv = [w.rearrange("p (c n) -> p c n", c=8) for w in w1c]

        for t in range(NT):
            stats_tile(t, True)
        tiny_ln(True)
        for t in range(NT):
            quant_tile(t, "deq")
            qb = t % 2
            bk_ = 6 + (t % 2)

            def tr_fn(e, s, qb=qb, bk_=bk_):
                for c in range(8):
                    e.transpose(pbf(bk_)[:, c * 128:(c + 1) * 128],
                                xq_tm[qb][:, c * 128:(c + 1) * 128],
                                eye_b[:]).then_inc(s["te"], 1)
            p.step("te", tr_fn, reads=[f"xq{qb}", "eye_b"],
                   writes=[f"pb{bk_}"], n_inc=8, tag="xdT")
            se(lambda e, bk_=bk_, t=t: e.copy(
                xqdTv[:, :, 128 * t + 1:128 * t + 129],
                pbf(bk_)[:, 0:1024].rearrange("p (c n) -> p c n", c=8)),
               [f"pb{bk_}"], ["xqdT"], tag="xdTe")
        hp = halo_col[:, 2 * call:2 * call + 2, :]
        ve(lambda e: e.tensor_copy(hp[:, 0, :], xqdTv[:, :, 1]), ["xqdT"],
           ["hpub"], tag="hpub")
        ve(lambda e: e.tensor_copy(hp[:, 1, :], xqdTv[:, :, 2048]),
           ["xqdT", "hpub"], ["hpub"], tag="hpub")
        dma("se", cb_halo_in[call], hp.rearrange("p a b -> p (a b)"),
            ["hpub"], [f"cbhi{call}"], "dA")
        p.step("ge", lambda e, s: e.collective_compute(
            "AllGather", ALU.bypass, replica_groups=[list(range(NCORES))],
            ins=[cb_halo_in[call].opt()], outs=[cb_halo_out[call].opt()])
            .then_inc(s["cc"], 1),
            reads=[f"cbhi{call}"], writes=[f"cbho{call}"], inc="cc", tag="cchalo")
        dma("se", halo_g[:], cb_halo_out[call].rearrange("(j p) k -> p j k", p=128),
            [f"cbho{call}", "halo_g"], ["halo_g"], "dA")
        ve(lambda e: e.memset(xqdTv[:, :, 0], 0.0), ["xqdT"], ["xqdT"])
        ve(lambda e: e.memset(xqdTv[:, :, 2049], 0.0), ["xqdT"], ["xqdT"])
        for j in range(NCORES):
            ve(lambda e, j=j: e.scalar_tensor_tensor(
                xqdTv[:, :, 0], halo_g[:, j, 8:16], masks_h[:, j:j + 1],
                xqdTv[:, :, 0], ALU.mult, ALU.add),
               ["halo_g", "masks_h", "xqdT"], ["xqdT"], tag="hselL")
            ve(lambda e, j=j: e.scalar_tensor_tensor(
                xqdTv[:, :, 2049], halo_g[:, j, 0:8], masks_h[:, 8 + j:8 + j + 1],
                xqdTv[:, :, 2049], ALU.mult, ALU.add),
               ["halo_g", "masks_h", "xqdT"], ["xqdT"], tag="hselR")
        sc_ = cur["sscr"]
        wCu2 = ar.view2(["wC0", "wC1"], 31, 512, BF16)
        wC2 = [wCu2[:, 0:512], wCu2[:, 512:1024]]
        for g in range(4):
            for m in range(32):
                par = m % 2
                dma("sy", w1cv[par], p_w1[:, :, m * 128:(m + 1) * 128]
                    .transpose([1, 0, 2]),
                    [f"w1c{par}", "pd_w1"], [f"w1c{par}"], "dS", tag="w1")
                bA = 0 if m % 2 == 0 else 2
                bB = bA + 1
                mm_group(bA, [(pb[bA][:, :], w1cv[par][:, dc, :],
                               xqdTv[:, dc, 512 * g:512 * g + 512])
                              for dc in range(8)],
                         [f"w1c{par}", "xqdT"], tag="mmA")
                mm_group(bB, [(pb[bB][:, 0:2], w1cv[par][:, dc, :],
                               xqdTv[:, dc, 512 * g + 512:512 * g + 514])
                              for dc in range(8)],
                         [f"w1c{par}", "xqdT"], tag="mmB")
                se(lambda e, bA=bA, m=m: e.activation(
                    sc_[:, 0:511], pb[bA][:, 1:512], ACT.Identity,
                    bias=taps[:, m, 3:4], scale=taps[:, m, 1:2]),
                   [f"pb{bA}", "taps_raw", "taps_b"], ["sscr"], tag="cv1")
                se(lambda e, bB=bB, m=m: e.activation(
                    sc_[:, 511:512], pb[bB][:, 0:1], ACT.Identity,
                    bias=taps[:, m, 3:4], scale=taps[:, m, 1:2]),
                   [f"pb{bB}", "taps_raw", "taps_b", "sscr"], ["sscr"], tag="cv1b")
                ve(lambda e, bA=bA, m=m: e.scalar_tensor_tensor(
                    sc_[:, 0:512], pb[bA][:, 0:512], taps[:, m, 0:1],
                    sc_[:, 0:512], ALU.mult, ALU.add),
                   [f"pb{bA}", "taps_raw", "sscr"], ["sscr"], tag="cv2")
                ve(lambda e, bA=bA, m=m: e.scalar_tensor_tensor(
                    sc_[:, 0:510], pb[bA][:, 2:512], taps[:, m, 2:3],
                    sc_[:, 0:510], ALU.mult, ALU.add),
                   [f"pb{bA}", "taps_raw", "sscr"], ["sscr"], tag="cv3")
                ve(lambda e, bB=bB, m=m: e.scalar_tensor_tensor(
                    sc_[:, 510:512], pb[bB][:, 0:2], taps[:, m, 2:3],
                    sc_[:, 510:512], ALU.mult, ALU.add),
                   [f"pb{bB}", "taps_raw", "sscr"], ["sscr"], tag="cv3b")
                se(lambda e, m=m: e.activation(h2gv[:, m, :], sc_[:, 0:512],
                                               ACT.Silu),
                   ["sscr"], ["h2g"], tag="cvsilu")
            for t in range(4 * g, 4 * g + 4):
                tc0 = (t % 4) * 128
                transpose_pack_f(
                    lambda c, tc0=tc0: h2gv[:, c, tc0:tc0 + 128], ["h2g"],
                    lambda pk, nb: h2tm[:, pk * 512:pk * 512 + nb * 128],
                    lambda pk: "h2tm", 32, [4, 5], bf=False, tag="h2tr")
                ve(lambda e, t=t: e.tensor_reduce(col("mmx", t), h2tm[:], AX.X,
                                                  ALU.max,
                                                  apply_absolute_value=True),
                   ["h2tm"], ["st_mmx"], tag="h2amax")
                ve(lambda e, t=t: e.tensor_scalar(col("t0", t), col("mmx", t),
                                                  1e-5, None, ALU.max),
                   ["st_mmx"], ["st_t0"])
                ve(lambda e, t=t: e.reciprocal(col("t1", t), col("t0", t)),
                   ["st_t0"], ["st_t1"])
                ve(lambda e, t=t: e.tensor_scalar(col("beta", t), col("t1", t),
                                                  127.0, None, ALU.mult),
                   ["st_t1"], ["st_beta"])
                ve(lambda e, t=t: e.tensor_scalar(col("dsc", t), col("t0", t),
                                                  mcol("w2"), None, ALU.mult),
                   ["st_t0", "ms_w2"], ["st_dsc"])
                ve(lambda e, t=t: e.tensor_scalar(col("dsc", t), col("dsc", t),
                                                  0.5 / 127.0, None, ALU.mult),
                   ["st_dsc"], ["st_dsc"])
                se(lambda e, t=t: e.activation(h2tm[:], h2tm[:], ACT.Identity,
                                               bias=RC, scale=col("beta", t)),
                   ["h2tm", "st_beta"], ["h2tm"], tag="h2q1")
                for q in range(4):
                    ve(lambda e, q=q: e.tensor_scalar(
                        h2qTu[:, q * 1024:(q + 1) * 1024],
                        h2tm[:, q * 1024:(q + 1) * 1024], -RC, None, ALU.add),
                       ["h2tm"], [f"h2qT{q}"], tag="h2q2")
                for q in range(4):
                    transpose_pack_f(
                        lambda c, q=q: h2qTu[:, q * 1024 + c * 128:
                                             q * 1024 + (c + 1) * 128],
                        [f"h2qT{q}"],
                        lambda pk, nb, q=q: h2qTu[:, q * 1024:q * 1024 + nb * 128],
                        lambda pk, q=q: f"h2qT{q}", 8, [4 + q % 2], bf=True,
                        tag="h2qtr")
                for h in range(2):
                    bd = 6 + h
                    for k in range(32):
                        par = k % 2
                        dma("sy", wC2[par], p_w2[k, :, h * 512:(h + 1) * 512],
                            [f"wC{par}", "pd_w2"], [f"wC{par}"], "dS", tag="w2")
                        mm_group(bd, [(pb[bd][:, :],
                                       h2qTu[:, k * 128:(k + 1) * 128], wC2[par])],
                                 [f"h2qT{k // 8}", f"wC{par}"], first=(k == 0),
                                 last=(k == 31), tag="mmw2")
                    ve(lambda e, t=t, h=h, bd=bd: e.scalar_tensor_tensor(
                        X[:, t, h * 512:(h + 1) * 512], pb[bd][:, :],
                        col("dsc", t), X[:, t, h * 512:(h + 1) * 512],
                        ALU.mult, ALU.add),
                       [f"pb{bd}", "st_dsc", f"x{t}"], [f"x{t}"], tag="xw2")

    def attn():
        qkT = ar.view("qkT", 0, 8192, BF16)
        qkTv = qkT.rearrange("p (t n) -> p t n", t=NT)
        xlnTb = ar.view2(["xlnTa0", "xlnTa1"], 8, 1024, BF16)
        xlnTa = [xlnTb[:, 0:1024], xlnTb[:, 1024:2048]]
        wq3 = {nm: ar.view(f"wq3_{nm}", 9 + 4 * i, 4096, BF16)
               for i, nm in enumerate("qkv")}
        wq3v = {nm: w.rearrange("p (c n) -> p c n", c=8) for nm, w in wq3.items()}
        ab = ar.view("ab", 21, 2048, BF16, rows=1)
        slotb_ = ar.view("slot0", 23, 1040, F32, rows=PR)
        slotb = [slotb_, slotb_]
        kvloc = ar.view("kvloc", 25, 1040, F32, rows=PR)
        kvbf = ar.view("kvbf", 32, 520, BF16)
        kvpair = kvbf[:, 0:8 * 130].rearrange("p (c n) -> p c n", c=8)
        vaugb = ar.view2(["vaug0", "vaug1"], 27, 1040, BF16)
        vaug = [vaugb[:, 0:1040], vaugb[:, 1040:2080]]
        vaugv = [w.rearrange("p (h n) -> p h n", h=NH) for w in vaug]
        kkernb = ar.view2(["kkern0", "kkern1"], 29, 1024, BF16)
        kkern = [kkernb[:, 0:1024], kkernb[:, 1024:2048]]
        ydivb = ar.view2(["ydiv0", "ydiv1"], 30, 1024, BF16)
        ydiv = [ydivb[:, 0:1024], ydivb[:, 1024:2048]]
        cur["sscr"] = ar.view("sscr", 31, 1024)
        for t in range(NT):
            stats_tile(t, False)
        tiny_ln(False)
        for i, nm in enumerate("qkvo"):
            dma("ge", ab[0:1, i * 1024:(i + 1) * 1024], a_b[nm], ["ab"], ["ab"], "dG")
        for nm in "qkv":
            dma("sy", wq3v[nm], p_attn[nm].transpose([1, 0, 2]),
                [f"wq3_{nm}", f"pd_a{nm}"], [f"wq3_{nm}"], "dS")
        ve(lambda e: e.memset(vaugv[0][:, :, 64], 1.0), ["vaug0"], ["vaug0"])
        ve(lambda e: e.memset(vaugv[1][:, :, 64], 1.0), ["vaug1"], ["vaug1"])
        kvoff = [(4, 0, 7), (5, 7, 14), (6, 14, 16)]
        pairoff = [(4, 0, 3), (5, 3, 6), (6, 6, 8)]
        for t in range(NT):
            qb = t % 2
            quant_tile(t, "ln")
            bk_ = 3 if t % 2 == 0 else 7

            def trx(e, s, qb=qb, bk_=bk_):
                for c in range(8):
                    e.transpose(pbf(bk_)[:, c * 128:(c + 1) * 128],
                                xq_tm[qb][:, c * 128:(c + 1) * 128],
                                eye_b[:]).then_inc(s["te"], 1)
            p.step("te", trx, reads=[f"xq{qb}", "eye_b"], writes=[f"pb{bk_}"],
                   n_inc=8, tag="xlnA")
            se(lambda e, bk_=bk_, qb=qb: e.copy(xlnTa[qb][:], pbf(bk_)[:, 0:1024]),
               [f"pb{bk_}"], [f"xlnTa{qb}"], tag="xlnAe")
            for ip, nm in enumerate("qkv"):
                b0, b1 = (0, 1) if (t + ip) % 2 == 0 else (2, 3)
                for h in range(2):
                    bn = b0 if h == 0 else b1
                    pairs = [(pb[bn][:, :], xlnTa[qb][:, dc * 128:(dc + 1) * 128],
                              wq3v[nm][:, dc, h * 512:(h + 1) * 512])
                             for dc in range(8)]
                    pairs.append((pb[bn][:, :], ones_b[:, :],
                                  ab[0:1, ip * 1024 + h * 512:
                                     ip * 1024 + (h + 1) * 512]))
                    mm_group(bn, pairs, [f"xlnTa{qb}", f"wq3_{nm}", "ab", "ones_b"],
                             tag=f"mm{nm}")
                if nm in ("q", "k"):
                    for h in range(2):
                        bn = b0 if h == 0 else b1
                        ve(lambda e, bn=bn, h=h: e.reduce_max(
                            dn_s[:, 8 * h:8 * h + 8],
                            pb[bn][:].rearrange("p (hh e) -> p hh e", hh=8),
                            AX.X, negate=True),
                           [f"pb{bn}"], ["dn_s"], tag="qmax")
                        ve(lambda e, bn=bn, h=h: e.tensor_tensor(
                            cur["sscr"][:, 512 * h:512 * h + 512]
                            .rearrange("p (hh e) -> p hh e", hh=8),
                            pb[bn][:].rearrange("p (hh e) -> p hh e", hh=8),
                            dn_s[:, 8 * h:8 * h + 8].unsqueeze(2)
                            .broadcast_to([128, 8, 64]), ALU.add),
                           [f"pb{bn}", "dn_s", "sscr"], ["sscr"], tag="qsub")
                    if nm == "q":
                        se(lambda e, qb=qb: e.activation(xq_tm[qb][:],
                                                         cur["sscr"][:], ACT.Exp),
                           ["sscr"], [f"xq{qb}"], tag="qexp")
                        bk2 = 3 if t % 2 == 1 else 7

                        def trq(e, s, qb=qb, bk2=bk2, t=t):
                            for c in range(8):
                                e.transpose(pbf(bk2)[:, c * 128:(c + 1) * 128],
                                            xq_tm[qb][:, c * 128:(c + 1) * 128],
                                            eye_b[:]).then_inc(s["te"], 1)
                        p.step("te", trq, reads=[f"xq{qb}", "eye_b"],
                               writes=[f"pb{bk2}"], n_inc=8, tag="qkT")
                        se(lambda e, bk2=bk2, t=t: e.copy(qkTv[:, t, :],
                                                          pbf(bk2)[:, 0:1024]),
                           [f"pb{bk2}"], ["qkT"], tag="qkTe")
                    else:
                        se(lambda e, qb=qb: e.activation(kkern[qb][:],
                                                         cur["sscr"][:], ACT.Exp),
                           ["sscr"], [f"kkern{qb}"], tag="kexp")
                else:
                    for h in range(2):
                        bn = b0 if h == 0 else b1
                        ve(lambda e, bn=bn, h=h, qb=qb: e.tensor_copy(
                            vaugv[qb][:, 8 * h:8 * h + 8, 0:64],
                            pb[bn][:].rearrange("p (hh e) -> p hh e", hh=8)),
                           [f"pb{bn}", f"vaug{qb}"], [f"vaug{qb}"], tag="vev")
            def kvacc(e, s, t=t, qb=qb):
                last = None
                for bank, h0, h1 in kvoff:
                    for h in range(h0, h1):
                        off = (h - h0) * 65
                        inst = e.matmul(pb[bank][0:64, off:off + 65],
                                        kkern[qb][:, h * 64:(h + 1) * 64],
                                        vaugv[qb][:, h, :],
                                        start=(t == 0 and h == h0),
                                        stop=(t == NT - 1 and h == h1 - 1))
                        last = inst
                        if t == NT - 1:
                            inst.then_inc(s["te"], 1)
                if t != NT - 1:
                    last.then_inc(s["te"], 1)
            p.step("te", kvacc, reads=[f"kkern{qb}", f"vaug{qb}"],
                   writes=["pb4", "pb5", "pb6"],
                   n_inc=(16 if t == NT - 1 else 1), tag="kvacc")
        if ATTN_STAGE < 2:
            return
        for bank, h0, h1 in kvoff:
            ve(lambda e, bank=bank, h0=h0, h1=h1: e.tensor_copy(
                kvloc[:, h0 * 65:h1 * 65], pb[bank][0:64, 0:(h1 - h0) * 65]),
               [f"pb{bank}"], ["kvloc"], tag="kvev")
        dma("se", cb_kv_in, kvloc[:], ["kvloc"], ["cbkvi"], "dA")
        p.step("ge", lambda e, s: e.collective_compute(
            "AllGather", ALU.bypass, replica_groups=[list(range(NCORES))],
            ins=[cb_kv_in.opt()], outs=[cb_kv_out.opt()]).then_inc(s["cc"], 1),
            reads=["cbkvi"], writes=["cbkvo"], inc="cc", tag="cckv")
        ve(lambda e: e.memset(kvloc[:], 0.0), ["kvloc", "cbkvi"], ["kvloc"])
        for j in range(NCORES):
            sb = slotb[j % 2]
            sk = "slot0"
            dma("se", sb[:], cb_kv_out[j * PR:(j + 1) * PR, :], ["cbkvo", sk],
                [sk], "dA")
            ve(lambda e, sb=sb, j=j: e.scalar_tensor_tensor(
                kvloc[:], sb[:], masks_kv[:, j:j + 1], kvloc[:],
                ALU.mult, ALU.add),
               [sk, "masks_kv", "kvloc"], ["kvloc"], tag="kvfold")
        ve(lambda e: e.memset(kvbf[:], 0.0), ["kvbf"], ["kvbf"], tag="kvz")
        kvlv = kvloc[:, 0:NH * 65].rearrange("p (c k n) -> p c k n", c=8, k=2)
        ve(lambda e: e.tensor_copy(kvpair[0:64, :, 0:65], kvlv[:, :, 0, :]),
           ["kvloc", "kvbf"], ["kvbf"], tag="kvcast")
        dma("ge", kvpair[64:128, :, 65:130], kvlv[:, :, 1, :],
            ["kvloc", "kvbf"], ["kvbf"], "dG", tag="kvdup", slow=True)
        if ATTN_STAGE < 3:
            return
        woT = ar.view("woT", 9, 4096, BF16)
        woTv = woT.rearrange("p (c n) -> p c n", c=8)
        dma("sy", woTv, p_attn["o"].transpose([1, 0, 2]), ["woT", "pd_ao"],
            ["woT"], "dS")
        for t in range(NT):
            qb = t % 2

            def ymm(e, s, t=t):
                for bank, c0, c1 in pairoff:
                    for c in range(c0, c1):
                        off = (c - c0) * 130
                        inst = e.matmul(
                            pb[bank][:, off:off + 130],
                            qkTv[:, t, c * 128:(c + 1) * 128],
                            kvpair[:, c, :],
                            start=(c == c0), stop=(c == c1 - 1))
                        inst.then_inc(s["te"], 1)
            p.step("te", ymm, reads=["qkT", "kvbf"],
                   writes=["pb4", "pb5", "pb6"], n_inc=8, tag="ymm")
            if ATTN_STAGE < 4:
                continue
            for bank, c0, c1 in pairoff:
                nc_ = c1 - c0
                ve(lambda e, bank=bank, c0=c0, nc_=nc_: e.tensor_scalar(
                    dn_s[:, 2 * c0:2 * (c0 + nc_)]
                    .rearrange("p (c k) -> p c k", k=2),
                    pb[bank][:, 0:nc_ * 130]
                    .rearrange("p (c k n) -> p c k n", c=nc_, k=2)[:, :, :, 64],
                    1e-6, None, ALU.add),
                   [f"pb{bank}"], ["dn_s"], tag="dnx")
            ve(lambda e: e.reciprocal(rd_s[:, :], dn_s[:, :]), ["dn_s"], ["rd_s"],
               tag="rdx")
            for bank, c0, c1 in pairoff:
                nc_ = c1 - c0
                ve(lambda e, bank=bank, c0=c0, nc_=nc_, qb=qb: e.tensor_tensor(
                    ydiv[qb][:, c0 * 128:(c0 + nc_) * 128]
                    .rearrange("p (c k e) -> p c k e", c=nc_, k=2),
                    pb[bank][:, 0:nc_ * 130]
                    .rearrange("p (c k n) -> p c k n", c=nc_, k=2)[:, :, :, 0:64],
                    rd_s[:, 2 * c0:2 * (c0 + nc_)]
                    .rearrange("p (c k) -> p c k", k=2).unsqueeze(3)
                    .broadcast_to([128, nc_, 2, 64]),
                    ALU.mult),
                   [f"pb{bank}", "rd_s", f"ydiv{qb}"], [f"ydiv{qb}"], tag="ydv")
            if ATTN_STAGE < 5:
                continue
            bk2 = 3 if t % 2 == 0 else 7

            def try_(e, s, qb=qb, bk2=bk2):
                for c in range(8):
                    e.transpose(pbf(bk2)[:, c * 128:(c + 1) * 128],
                                ydiv[qb][:, c * 128:(c + 1) * 128],
                                eye_b[:]).then_inc(s["te"], 1)
            p.step("te", try_, reads=[f"ydiv{qb}", "eye_b"], writes=[f"pb{bk2}"],
                   n_inc=8, tag="ydT")
            se(lambda e, bk2=bk2, qb=qb: e.copy(xq_tm[qb][:], pbf(bk2)[:, 0:1024]),
               [f"pb{bk2}"], [f"xq{qb}"], tag="ydTe")
            if ATTN_STAGE < 6:
                continue
            for h in range(2):
                bn = h + 2 * (t % 2)
                pairs = [(pb[bn][:, :], xq_tm[qb][:, dc * 128:(dc + 1) * 128],
                          woTv[:, dc, h * 512:(h + 1) * 512]) for dc in range(8)]
                pairs.append((pb[bn][:, :], ones_b[:, :],
                              ab[0:1, 3 * 1024 + h * 512:3 * 1024 + (h + 1) * 512]))
                mm_group(bn, pairs, [f"xq{qb}", "woT", "ab", "ones_b"], tag="mmo")
                ve(lambda e, t=t, h=h, bn=bn: e.scalar_tensor_tensor(
                    X[:, t, h * 512:(h + 1) * 512], pb[bn][:, :], 1.0,
                    X[:, t, h * 512:(h + 1) * 512], ALU.mult, ALU.add),
                   [f"pb{bn}", f"x{t}"], [f"x{t}"], tag="xattn")

    def store_output():
        dma("se", out_t.rearrange("(t p) d -> p t d", p=128), X[:],
            [f"x{t}" for t in range(NT)], ["outb"], "dA")
        p.step("se", lambda e, s: e.wait_ge(s["dA"], p.count["dA"]),
               reads=["outb"], writes=[], n_inc=0)
        p.step("ge", lambda e, s: e.wait_ge(s["dA"], p.count["dA"]),
               reads=["outb"], writes=[], n_inc=0)

    prep_quant_resident(w_gate, p_gate, "gate", INNER, D, "g")
    prep_quant_resident(w_up, p_up, "up", INNER, D, "u")
    prep_quant_resident(w_down, p_down, "down", D, INNER, "d")
    if 2 in SUBLAYERS:
        prep_ssm()
    if 4 in SUBLAYERS or 6 in SUBLAYERS:
        prep_quant_resident(w1_d, p_w1, "w1", INNER, D, "w1")
        prep_quant_resident(w2_d, p_w2, "w2", D, INNER, "w2")
        prep_taps()
    if 5 in SUBLAYERS:
        for nm in "qkvo":
            prep_attn_w(a_w[nm], p_attn[nm], f"a{nm}")

    if DBG:
        dma("se", dbg_pg, p_gate[:, :, 0:512], ["pd_g"], ["dbg1"], "dA")
    if 1 in SUBLAYERS:
        ffn0()
    if DBG:
        dma("se", dbg_msw, msw[:], ["ms_gate", "ms_up", "ms_down"], ["dbg2"], "dA")
        dma("se", dbg_stats, stats[:], ["st_aa", "st_bb", "st_rs", "st_mmx",
                                        "st_beta", "st_dsc"], ["dbg3"], "dA")
    if 2 in SUBLAYERS:
        ssm()
    if 3 in SUBLAYERS:
        ffn0()
    if 4 in SUBLAYERS:
        l1ffn(0)
    if 5 in SUBLAYERS:
        attn()
    if 6 in SUBLAYERS:
        l1ffn(1)
    store_output()

    p.emit(sems)
    nc.finalize()
    ctx.close()
    return nc


_CACHE = {}


def _get_nc():
    if "nc" not in _CACHE:
        _CACHE["nc"] = build_program()
    return _CACHE["nc"]


def _shard_inputs(inputs):
    f32 = np.float32
    x = np.asarray(inputs["L0_attn_x"], f32)
    common = {
        "w_toP": np.ascontiguousarray(inputs["L0_attn_W_toP"], dtype=f32),
        "w_U": np.ascontiguousarray(np.asarray(inputs["L0_attn_W_U"], f32).reshape(D, P5 * 4)),
        "w_F": np.ascontiguousarray(inputs["L0_attn_W_F"], dtype=f32),
        "w_V": np.ascontiguousarray(np.asarray(inputs["L0_attn_W_V"], f32).reshape(1, P5 * 4)),
        "w_lam": np.ascontiguousarray(inputs["L0_attn_W_lam"], dtype=f32),
        "b_mat": np.ascontiguousarray(inputs["L0_attn_B_mat"], dtype=f32),
        "c_mat": np.ascontiguousarray(inputs["L0_attn_C_mat"], dtype=f32),
        "w_fromP": np.ascontiguousarray(inputs["L0_attn_W_fromP"], dtype=f32),
        "w_gate": np.ascontiguousarray(inputs["L0_ffn_w_gate"], dtype=f32),
        "w_up": np.ascontiguousarray(inputs["L0_ffn_w_up"], dtype=f32),
        "w_down": np.ascontiguousarray(inputs["L0_ffn_w_down"], dtype=f32),
        "w1": np.ascontiguousarray(inputs["L1_ffn_w1"], dtype=f32),
        "w2": np.ascontiguousarray(inputs["L1_ffn_w2"], dtype=f32),
        "conv_w": np.ascontiguousarray(np.asarray(inputs["L1_ffn_conv_w"], f32).reshape(INNER, 3)),
        "conv_b": np.ascontiguousarray(np.asarray(inputs["L1_ffn_conv_b"], f32).reshape(1, INNER)),
        "eye_bf": np.eye(128, dtype=BFNP),
        "eye_f32": np.eye(128, dtype=f32),
        "cpool_d": np.tile(np.array([[RC, 1e-5, 0.0, 1.0]], f32), (128, 1)),
    }
    for nm in "qkvo":
        common[f"W{nm}"] = np.ascontiguousarray(inputs[f"L1_attn_W{nm}"], dtype=f32)
        common[f"b{nm}"] = np.ascontiguousarray(
            np.asarray(inputs[f"L1_attn_b{nm}"], f32).reshape(1, D))
    maps = []
    for c in range(NCORES):
        b, t0 = c // 4, (c % 4) * TOK
        m = dict(common)
        m["x_in"] = np.ascontiguousarray(x[b, t0:t0 + TOK, :])
        sm = np.zeros((PR, 16), f32)
        km = np.zeros((PR, 8), f32)
        hm = np.zeros((128, 16), f32)
        for j in range(NCORES):
            same = (j // 4) == (c // 4)
            sm[:, j] = 1.0 if (same and j < c) else 0.0
            sm[:, 8 + j] = 1.0 - sm[0, j]
            km[:, j] = 1.0 if same else 0.0
            hm[:, j] = 1.0 if (same and j == c - 1) else 0.0
            hm[:, 8 + j] = 1.0 if (same and j == c + 1) else 0.0
        m["scanmask"], m["kvmask"], m["halomask"] = sm, km, hm
        maps.append(m)
    return maps


def kernel(**inputs):
    nc = _get_nc()
    maps = _shard_inputs(inputs)
    res = run_bass_kernel_spmd(nc, maps, core_ids=list(range(NCORES)))
    out = np.empty((B, T, D), np.float32)
    for c in range(NCORES):
        b, t0 = c // 4, (c % 4) * TOK
        out[b, t0:t0 + TOK, :] = res.results[c]["out"]
    return out


# version 02
# speedup vs baseline: 1.3397x; 1.3397x over previous
import contextlib
import numpy as np
import ml_dtypes
import concourse.bass as bass
import concourse.mybir as mybir
from concourse.bass_utils import run_bass_kernel_spmd

F32 = mybir.dt.float32
BF16 = mybir.dt.bfloat16
ALU = mybir.AluOpType
ACT = mybir.ActivationFunctionType
AX = mybir.AxisListType
BFNP = ml_dtypes.bfloat16

NCORES = 8
B, T, D = 2, 8192, 1024
TOK, NT = 2048, 16
INNER = 4096
P5, PR = 512, 64
NH, HD = 16, 64
RC = 3.0 * 2.0 ** 22
NEG = -3.0e38
UN = 1024
NUNITS = 34

SUBLAYERS = (1, 2, 3, 4, 5, 6)
ATTN_STAGE = 6
DBG = False


class Prog:
    def __init__(self, nc):
        self.nc = nc
        self.order = ("sy", "te", "ve", "se", "ge")
        self.steps = {s: [] for s in self.order}
        self.count = dict.fromkeys(["te", "ve", "se", "ge", "sy",
                                    "dS", "dA", "dG", "cc"], 0)
        self.bufs = {}
        self.waited = {s: {} for s in self.order}
        self.pending = {}

    def _merge(self, toks, tok):
        if tok is None:
            return
        sem, val = tok
        if toks.get(sem, 0) < val:
            toks[sem] = val

    def step(self, st, fn, reads=(), writes=(), inc=None, n_inc=1, tag=""):
        inc = inc or st
        if getattr(self, "boot", False):
            reads = list(reads) + ["boot"]
        toks = {}
        for k in reads:
            b = self.bufs.get(k)
            if b:
                self._merge(toks, b["w"])
        for k in writes:
            for tok in self.pending.pop(k, ()):
                self._merge(toks, tok)
            b = self.bufs.get(k)
            if b:
                self._merge(toks, b["w"])
                for tok in b["r"]:
                    self._merge(toks, tok)
        waits = []
        for sem, val in toks.items():
            if self.waited[st].get(sem, 0) < val:
                waits.append((sem, val))
                self.waited[st][sem] = val
        unit = 16 if inc in ("dS", "dA", "dG") else 1
        if n_inc:
            self.count[inc] += unit * n_inc
            tok = (inc, self.count[inc])
            for k in reads:
                self.bufs.setdefault(k, {"w": None, "r": []})["r"].append(tok)
            for k in writes:
                self.bufs[k] = {"w": tok, "r": []}
        self.steps[st].append((waits, fn, tag))

    def barrier_tokens(self, keys):
        toks = {}
        for k in keys:
            b = self.bufs.get(k)
            if b:
                self._merge(toks, b["w"])
                for t in b["r"]:
                    self._merge(toks, t)
        return list(toks.items())

    def emit(self, sems):
        with self.nc.Block() as blk:
            for st in self.order:
                steps = self.steps[st]

                def run(eng, steps=steps):
                    for waits, fn, tag in steps:
                        for sem, val in waits:
                            eng.wait_ge(sems[sem], val)
                        fn(eng, sems)

                getattr(blk, {"te": "tensor", "ve": "vector", "se": "scalar",
                              "ge": "gpsimd", "sy": "sync"}[st])(run)


class Arena:

    def __init__(self, p, t):
        self.p = p
        self.t = t
        self.unit_keys = [set() for _ in range(NUNITS)]

    def view(self, key, u0, ncols_f32, dtype=F32, row0=0, rows=128):
        return self.view2([key], u0, ncols_f32, dtype, row0, rows)

    def view2(self, keys, u0, ncols_f32, dtype=F32, row0=0, rows=128):
        ulast = u0 + (ncols_f32 + UN - 1) // UN
        old = set()
        for u in range(u0, ulast):
            old |= self.unit_keys[u]
        toks = self.p.barrier_tokens(old - set(keys))
        if toks:
            for key in keys:
                self.p.pending.setdefault(key, []).extend(list(toks))
                self.p.bufs.pop(key, None)
        for u in range(u0, ulast):
            self.unit_keys[u] = set(keys)
        ap = self.t[row0:row0 + rows, u0 * UN:u0 * UN + ncols_f32]
        if dtype == BF16:
            ap = ap.bitcast(BF16)
        return ap


def build_program():
    nc = bass.Bass("TRN2", target_bir_lowering=False, debug=False,
                   dynamic_dma_scratch_size=12288)
    ctx = contextlib.ExitStack()

    def ein(name, shape, dtype=F32):
        return nc.dram_tensor(name, list(shape), dtype, kind="ExternalInput").ap()

    x_in = ein("x_in", [TOK, D])
    w_toP = ein("w_toP", [D, P5])
    w_U = ein("w_U", [D, P5 * 4])
    w_F = ein("w_F", [D, P5])
    w_V = ein("w_V", [1, P5 * 4])
    w_lam = ein("w_lam", [P5, PR])
    b_mat = ein("b_mat", [P5, PR])
    c_mat = ein("c_mat", [PR, P5])
    w_fromP = ein("w_fromP", [P5, D])
    w_gate = ein("w_gate", [INNER, D])
    w_up = ein("w_up", [INNER, D])
    w_down = ein("w_down", [D, INNER])
    w1_d = ein("w1", [INNER, D])
    w2_d = ein("w2", [D, INNER])
    conv_w = ein("conv_w", [INNER, 3])
    conv_b = ein("conv_b", [1, INNER])
    a_w = {nm: ein(f"W{nm}", [D, D]) for nm in "qkvo"}
    a_b = {nm: ein(f"b{nm}", [1, D]) for nm in "qkvo"}
    eye_bf_d = ein("eye_bf", [128, 128], BF16)
    eye_f32_d = ein("eye_f32", [128, 128])
    scanmask_d = ein("scanmask", [PR, 16])
    kvmask_d = ein("kvmask", [PR, 8])
    halomask_d = ein("halomask", [128, 16])
    cpool_d = ein("cpool_d", [128, 4])
    out_t = nc.dram_tensor("out", [TOK, D], F32, kind="ExternalOutput").ap()
    if DBG:
        dbg_msw = nc.dram_tensor("dbg_msw", [128, 12], F32, kind="ExternalOutput").ap()
        dbg_stats = nc.dram_tensor("dbg_stats", [128, 20, NT], F32, kind="ExternalOutput").ap()
        dbg_xq = nc.dram_tensor("dbg_xq", [128, D], BF16, kind="ExternalOutput").ap()
        dbg_actT = nc.dram_tensor("dbg_actT", [128, 1024], BF16, kind="ExternalOutput").ap()
        dbg_mid = nc.dram_tensor("dbg_mid", [128, INNER], F32, kind="ExternalOutput").ap()
        dbg_pg = nc.dram_tensor("dbg_pg", [8, 128, 512], BF16, kind="ExternalOutput").ap()
        dbg_st2 = nc.dram_tensor("dbg_st2", [128, 20, NT], F32, kind="ExternalOutput").ap()
        dbg_mq = nc.dram_tensor("dbg_mq", [128, INNER], BF16, kind="ExternalOutput").ap()

    def idram(name, shape, dtype):
        return nc.dram_tensor(name, list(shape), dtype).ap()

    p_gate = idram("p_gate", [32, 128, 1024], BF16)
    p_up = idram("p_up", [32, 128, 1024], BF16)
    p_w1 = idram("p_w1", [32, 128, 1024], BF16)
    p_down = idram("p_down", [32, 128, D], BF16)
    p_w2 = idram("p_w2", [32, 128, D], BF16)
    p_attn = {nm: idram(f"p_W{nm}", [8, 128, D], BF16) for nm in "qkvo"}
    p_weff = idram("p_weff", [8, 128, P5], BF16)
    p_wF = idram("p_wF", [8, 128, P5], BF16)
    p_fromP = idram("p_fromP", [4, 128, D], BF16)
    p_cw = idram("p_cw", [PR, D], BF16)
    cb_scan_in = idram("cb_scan_in", [PR, 2], F32)
    cb_scan_out = idram("cb_scan_out", [NCORES * PR, 2], F32)
    cb_kv_in = idram("cb_kv_in", [PR, NH * (HD + 1)], F32)
    cb_kv_out = idram("cb_kv_out", [NCORES * PR, NH * (HD + 1)], F32)
    cb_halo_in = [idram(f"cb_halo_in{i}", [128, 16], BF16) for i in range(2)]
    cb_halo_out = [idram(f"cb_halo_out{i}", [NCORES * 128, 16], BF16) for i in range(2)]

    p = Prog(nc)

    def sbuf(name, shape, dtype):
        return ctx.enter_context(nc.sbuf_tensor(name, list(shape), dtype))

    X = sbuf("X", [128, NT, D], F32)
    AR = sbuf("AR", [128, NUNITS * UN], F32)
    ar = Arena(p, AR)
    eye_b = sbuf("eye_b", [128, 128], BF16)
    eye_f = sbuf("eye_f", [128, 128], F32)
    ones_b = sbuf("ones_b", [1, 128], BF16)
    ones_rf = sbuf("ones_rf", [1, 128], F32)
    ones_cf = sbuf("ones_cf", [128, 1], F32)
    stats = sbuf("stats", [128, 20, NT], F32)
    ST = {k: i for i, k in enumerate(
        ["sum", "sumsq", "mx", "mn", "m", "var", "istd", "aa", "bb", "rs",
         "gsc", "usc", "mmx", "mmn", "beta", "dsc", "t0", "t1", "t2", "t3"])}
    msw = sbuf("msw", [128, 12], F32)
    MS = {k: i for i, k in enumerate(
        ["gate", "up", "down", "w1", "w2", "gate_r", "up_r", "down_r",
         "w1_r", "w2_r", "tmp", "tmp2"])}
    colscr = sbuf("colscr", [128, 40], F32)
    taps = sbuf("taps", [128, 32, 4], F32)
    masks_s = sbuf("masks_s", [PR, 16], F32)
    masks_kv = sbuf("masks_kv", [PR, 8], F32)
    masks_h = sbuf("masks_h", [128, 16], F32)
    wlam_s = sbuf("wlam_s", [128, 4, PR], BF16)
    bmat_s = sbuf("bmat_s", [128, 4, PR], BF16)
    cmT_s = sbuf("cmT_s", [128, 4, PR], BF16)
    carry = sbuf("carry", [PR, 1], F32)
    scAB = sbuf("scAB", [PR, 4], F32)
    gath = sbuf("gath", [PR, 16], F32)
    dn_s = sbuf("dn_s", [128, NH], F32)
    rd_s = sbuf("rd_s", [128, NH], F32)
    halo_g = sbuf("halo_g", [128, 8, 16], BF16)
    halo_col = sbuf("halo_col", [128, 4, 8], BF16)
    xq_tm = [sbuf(f"xq_tm{i}", [128, D], BF16) for i in range(2)]
    pubs = sbuf("pubs", [PR, 2], F32)

    cpool = sbuf("cpool", [128, 4], F32)
    nc.const_aps.register(F32, RC, cpool[:, 0:1])
    nc.const_aps.register(F32, 1e-5, cpool[:, 1:2])
    nc.const_aps.register(F32, 0.0, cpool[:, 2:3])
    nc.const_aps.register(F32, 1.0, cpool[:, 3:4])

    pb = [ctx.enter_context(nc.psum_tensor(f"pb{i}", [128, 512], F32)) for i in range(8)]

    def pbf(i):
        return pb[i][:].bitcast(BF16)

    semnames = ["te", "ve", "se", "ge", "sy", "dS", "dA", "dG", "cc"]
    sems = {n: ctx.enter_context(nc.semaphore(f"sem_{n}")) for n in semnames}

    cur = {}

    def dma(st_, dst, src, reads, writes, inc, tag="", slow=False):
        p.step(st_, lambda e, s, d=dst, sr=src: e.dma_start(
            out=d, in_=sr, allow_slow_non_contiguous=slow)
               .then_inc(s[inc], 16), reads=reads, writes=writes, inc=inc, tag=tag)

    def ve(fn, reads, writes, tag=""):
        p.step("ve", lambda e, s: fn(e).then_inc(s["ve"], 1), reads=reads,
               writes=writes, tag=tag)

    def ge(fn, reads, writes, tag=""):
        p.step("ge", lambda e, s: fn(e).then_inc(s["ge"], 1), reads=reads,
               writes=writes, tag=tag)

    def se(fn, reads, writes, tag=""):
        p.step("se", lambda e, s: fn(e).then_inc(s["se"], 1), reads=reads,
               writes=writes, tag=tag)

    def col(name, t):
        return stats[:, ST[name], t:t + 1]

    def cols(name, lo, hi):
        return stats[:, ST[name], lo:hi]

    def mcol(name):
        return msw[:, MS[name]:MS[name] + 1]

    def mm_group(bank, pairs, reads, tag="", first=True, last=True):
        def fn(e, s):
            n = len(pairs)
            for i, (o, l, r) in enumerate(pairs):
                inst = e.matmul(o, l, r, start=(first and i == 0),
                                stop=(last and i == n - 1))
                if i == n - 1:
                    inst.then_inc(s["te"], 1)
        p.step("te", fn, reads=reads, writes=[f"pb{bank}"], tag=tag)

    def transpose_pack_f(block_fn, src_keys, dst_fn, dst_keys, nblocks, banks,
                         bf=True, tag=""):
        per = 8 if bf else 4
        idt, idk = (eye_b, "eye_b") if bf else (eye_f, "eye_f")
        view = pbf if bf else (lambda b: pb[b][:])
        for pk in range((nblocks + per - 1) // per):
            bk_ = banks[pk % len(banks)]
            blocks = list(range(pk * per, min(pk * per + per, nblocks)))

            def tr_fn(e, s, blocks=blocks, bk_=bk_):
                for j, c in enumerate(blocks):
                    e.transpose(view(bk_)[:, j * 128:(j + 1) * 128],
                                block_fn(c), idt[:]).then_inc(s["te"], 1)
            p.step("te", tr_fn, reads=list(src_keys) + [idk],
                   writes=[f"pb{bk_}"], n_inc=len(blocks), tag=tag)
            nb = len(blocks)
            se(lambda e, pk=pk, nb=nb, bk_=bk_: e.copy(
                dst_fn(pk, nb), view(bk_)[:, :nb * 128]),
               [f"pb{bk_}"], [dst_keys(pk)], tag=tag)

    def transpose_pack(src_ap, src_keys, dst_ap, dst_key, nblocks, banks,
                       bf=True, tag=""):
        per = 8 if bf else 4
        transpose_pack_f(lambda c: src_ap[:, c * 128:(c + 1) * 128], src_keys,
                         lambda pk, nb: dst_ap[:, pk * per * 128:
                                               pk * per * 128 + nb * 128],
                         lambda pk: dst_key, nblocks, banks, bf=bf, tag=tag)

    dma("se", eye_b[:], eye_bf_d, [], ["eye_b"], "dA")
    dma("se", eye_f[:], eye_f32_d, [], ["eye_f"], "dA")
    dma("se", masks_s[:], scanmask_d, [], ["masks_s"], "dA")
    dma("se", masks_kv[:], kvmask_d, [], ["masks_kv"], "dA")
    dma("se", masks_h[:], halomask_d, [], ["masks_h"], "dA")
    ve(lambda e: e.memset(ones_b[:], 1.0), [], ["ones_b"])
    ve(lambda e: e.memset(ones_rf[:], 1.0), [], ["ones_rf"])
    ve(lambda e: e.memset(ones_cf[:], 1.0), [], ["ones_cf"])
    dma("se", cpool[:], cpool_d, [], ["boot"], "dA")
    p.boot = True
    dma("se", X[:], x_in.rearrange("(t p) d -> p t d", p=128),
        [], [f"x{t}" for t in range(NT)], "dA")

    def prep_quant_resident(w_dram, p_dram, mkey, rows, colsn, tag):
        nt_ = rows // 128
        wres = ar.view(f"wres_{tag}", 0, nt_ * colsn)
        fscr = ar.view(f"fscr_{tag}", 32, 1024)
        bs2 = ar.view(f"bs2_{tag}", 33, 1024)
        bscr = bs2[:, 0:512].bitcast(BF16)
        stage = bs2[:, 512:1024].bitcast(BF16)
        acc_k = f"pacc_{tag}"
        for i in range(nt_):
            dma("sy", wres[:, i * colsn:(i + 1) * colsn],
                w_dram[i * 128:(i + 1) * 128, :],
                [f"wres_{tag}"] if i == 0 else [], [f"wres_{tag}"], "dS",
                tag=f"p{tag}")
            ve(lambda e, i=i: e.reduce_sum(colscr[:, i:i + 1],
                                           wres[:, i * colsn:(i + 1) * colsn],
                                           AX.X, apply_absolute_value=True),
               [f"wres_{tag}"], [acc_k])
        ve(lambda e: e.reduce_sum(colscr[:, 32:33], colscr[:, 0:nt_], AX.X),
           [acc_k], [acc_k])
        mm_group(7, [(pb[7][0:1, 0:1], colscr[:, 32:33], ones_cf[:, :])],
                 [acc_k, "ones_cf"], tag=f"ps{tag}")
        ve(lambda e: e.tensor_scalar(colscr[0:1, 33:34], pb[7][0:1, 0:1],
                                     1.0 / (rows * colsn), 1e-5, ALU.mult, ALU.max),
           ["pb7"], ["ptiny"])
        mm_group(7, [(pb[7][:, 0:1], ones_rf[:, :], colscr[0:1, 33:34])],
                 ["ptiny", "ones_rf"], tag=f"pb{tag}")
        ve(lambda e: e.tensor_copy(mcol(mkey), pb[7][:, 0:1]), ["pb7"], [f"ms_{mkey}"])
        ve(lambda e: e.reciprocal(mcol(mkey + "_r"), mcol(mkey)),
           [f"ms_{mkey}"], [f"ms_{mkey}_r"])
        nsl = (colsn + 1023) // 1024
        for i in range(nt_):
            for sl in range(nsl):
                c0 = sl * 1024
                cw = min(1024, colsn - c0)
                src = wres[:, i * colsn + c0: i * colsn + c0 + cw]
                se(lambda e, src=src, cw=cw: e.activation(
                    fscr[:, :cw], src, ACT.Identity, bias=RC,
                    scale=mcol(mkey + "_r")),
                   [f"wres_{tag}", f"ms_{mkey}_r"], [f"fscr_{tag}"])
                ve(lambda e, cw=cw: e.tensor_scalar(fscr[:, :cw], fscr[:, :cw],
                                                    -RC, None, ALU.add),
                   [f"fscr_{tag}"], [f"fscr_{tag}"])
                ve(lambda e, cw=cw: e.tensor_scalar(
                    bscr[:, :cw], fscr[:, :cw], 1.0, -1.0, ALU.min, ALU.max),
                   [f"fscr_{tag}"], [f"bscr_{tag}"])
                nb = cw // 128
                transpose_pack(bscr, [f"bscr_{tag}"], stage, f"stage_{tag}",
                               nb, [6, 7], tag=f"pt{tag}")
                if colsn == 1024:
                    def st_fn(e, s, i=i, nb=nb):
                        e.dma_start(out=p_dram[i, :, 0:nb * 128],
                                    in_=stage[:, :nb * 128]).then_inc(s["dS"], 16)
                else:
                    dc0 = c0 // 128

                    def st_fn(e, s, dc0=dc0, nb=nb, i=i):
                        e.dma_start(
                            out=p_dram[dc0:dc0 + nb, :, i * 128:(i + 1) * 128]
                            .transpose([1, 0, 2]),
                            in_=stage[:, :nb * 128].rearrange(
                                "p (c r) -> p c r", c=nb),
                            allow_slow_non_contiguous=True).then_inc(s["dS"], 16)
                p.step("sy", st_fn, reads=[f"stage_{tag}"], writes=[f"pd_{tag}"],
                       inc="dS", tag=f"po{tag}")

    def prep_attn_w(w_dram, p_dram, tag, u0=32):
        braw = ar.view(f"braw_{tag}", u0, 1024, BF16)
        stg = ar.view(f"stg_{tag}", u0 + 1, 512, BF16)
        for i in range(8):
            dma("ge", braw[:, 0:1024], w_dram[i * 128:(i + 1) * 128, :],
                [f"braw_{tag}"], [f"braw_{tag}"], "dG", tag=f"pa{tag}")
            transpose_pack(braw, [f"braw_{tag}"], stg, f"stg_{tag}", 8, [6, 7],
                           tag=f"pa{tag}")

            def st_fn(e, s, i=i):
                e.dma_start(out=p_dram[:, :, i * 128:(i + 1) * 128]
                            .transpose([1, 0, 2]),
                            in_=stg[:, 0:1024].rearrange("p (c r) -> p c r", c=8)) \
                    .then_inc(s["dS"], 16)
            p.step("sy", st_fn, reads=[f"stg_{tag}"], writes=[f"pd_{tag}"], inc="dS")

    def prep_cast(w_dram, p_dram, nchunks, colsn, tag, u0=32):
        braw = ar.view(f"braw_{tag}", u0, 1024, BF16)
        for i in range(nchunks):
            dma("ge", braw[:, 0:colsn], w_dram[i * 128:(i + 1) * 128, :],
                [f"braw_{tag}"], [f"braw_{tag}"], "dG")
            dma("sy", p_dram[i, :, :], braw[:, 0:colsn], [f"braw_{tag}"],
                [f"pd_{tag}"], "dS")

    def prep_ssm():
        wus = ar.view("wus", 0, 16 * UN)
        wvb = ar.view("wvb", 16, 2048)
        escr = ar.view("escr", 18, 1024)
        wtp = ar.view("wtp", 19, 512)
        dma("se", wvb[0:1, :], w_V, ["wvb"], ["wvb_row"], "dA")
        for q in range(4):
            mm_group(5, [(pb[5][:, :], ones_rf[:, :], wvb[0:1, q * 512:(q + 1) * 512])],
                     ["wvb_row", "ones_rf"], tag="wvb")
            ve(lambda e, q=q: e.tensor_copy(wvb[:, q * 512:(q + 1) * 512], pb[5][:, :]),
               ["pb5"], ["wvb"])
        for dc in range(8):
            dma("sy", wus[:, dc * 2048:(dc + 1) * 2048],
                w_U[dc * 128:(dc + 1) * 128, :],
                ["wus"] if dc == 0 else [], ["wus"], "dS")
        for dc in range(8):
            wuv = wus[:, dc * 2048:(dc + 1) * 2048].rearrange("p (q a) -> p q a", a=4)
            wvv = wvb[:].rearrange("p (q a) -> p q a", a=4)
            dma("se", wtp[:, :], w_toP[dc * 128:(dc + 1) * 128, :],
                ["wtp"], ["wtp"], "dA")
            ve(lambda e, wuv=wuv, wvv=wvv: e.tensor_mul(
                escr[:, 0:512], wuv[:, :, 0], wvv[:, :, 0]),
               ["wus", "wvb"], ["escr0"])
            for a in (1, 2, 3):
                ve(lambda e, a=a, wuv=wuv, wvv=wvv: e.tensor_mul(
                    escr[:, 512:1024], wuv[:, :, a], wvv[:, :, a]),
                   ["wus", "wvb"], ["escr1"])
                ve(lambda e: e.tensor_add(escr[:, 0:512], escr[:, 0:512],
                                          escr[:, 512:1024]),
                   ["escr0", "escr1"], ["escr0"])
            ve(lambda e: e.tensor_add(escr[:, 512:1024].bitcast(BF16)[:, 0:512],
                                      escr[:, 0:512], wtp[:, :]),
               ["escr0", "wtp"], ["escr1"])
            dma("sy", p_weff[dc, :, :], escr[:, 512:1024].bitcast(BF16)[:, 0:512],
                ["escr1"], ["pd_weff"], "dS")
        prep_cast(w_F, p_wF, 8, P5, "wF", u0=17)
        prep_cast(w_fromP, p_fromP, 4, D, "fromP", u0=17)
        dma("ge", wlam_s[:], w_lam.rearrange("(f p) o -> p f o", p=128),
            [], ["wlam_s"], "dG")
        dma("ge", bmat_s[:], b_mat.rearrange("(f p) o -> p f o", p=128),
            [], ["bmat_s"], "dG")
        craw = ar.view("craw", 19, 256, BF16)
        dma("ge", craw[0:64, 0:512], c_mat, ["craw"], ["craw"], "dG")

        def trc(e, s):
            for j in range(4):
                e.transpose(pbf(5)[:, j * 64:(j + 1) * 64],
                            craw[0:64, j * 128:(j + 1) * 128],
                            eye_b[0:64, 0:64]).then_inc(s["te"], 1)
        p.step("te", trc, reads=["craw", "eye_b"], writes=["pb5"], n_inc=4)
        ve(lambda e: e.tensor_copy(cmT_s[:].rearrange("p f o -> p (f o)"),
                                   pbf(5)[:, 0:256]), ["pb5"], ["cmT_s"])
        cwc = ar.view("cwc", 20, 512, BF16)
        for pc in range(4):
            dma("sy", cwc[:, 0:1024], p_fromP[pc, :, :], ["cwc", "pd_fromP"], ["cwc"], "dS")
            for h in range(2):
                mm_group(3 + h, [(pb[3 + h][0:64, :], cmT_s[:, pc, :],
                                  cwc[:, h * 512:(h + 1) * 512])],
                         ["cwc", "cmT_s"], first=(pc == 0), last=(pc == 3),
                         tag="cw")
        cwo = ar.view("cwo", 21, 512, BF16)
        for h in range(2):
            ve(lambda e, h=h: e.tensor_copy(cwo[0:64, h * 512:(h + 1) * 512],
                                            pb[3 + h][0:64, :]),
               [f"pb{3 + h}"], ["cwo"])
        dma("sy", p_cw, cwo[0:64, 0:1024], ["cwo"], ["pd_cw"], "dS")

    def prep_taps():
        dma("se", taps[:, :, 0:3],
            conv_w.rearrange("(m p) k -> p m k", p=128), [], ["taps_raw"], "dA")
        dma("se", taps[:, :, 3:4], conv_b.rearrange("o (m p) -> p m o", p=128),
            [], ["taps_b"], "dA", slow=True)
        for k in range(3):
            ve(lambda e, k=k: e.tensor_scalar(taps[:, :, k], taps[:, :, k],
                                              mcol("w1"), None, ALU.mult),
               ["taps_raw", "ms_w1"], ["taps_raw"])

    def stats_tile(t, quant):
        xk = f"x{t}"
        ve(lambda e: e.reduce_sum(col("sum", t), X[:, t, :], AX.X), [xk], ["st_sum"])
        se(lambda e: e.activation(cur["sscr"][:, 0:1024], X[:, t, :], ACT.Square,
                                  accum_out=col("sumsq", t)),
           [xk], ["sscr", "st_sumsq"])
        if quant:
            ve(lambda e: e.reduce_max(col("mx", t), X[:, t, :], AX.X), [xk], ["st_mx"])
            ve(lambda e: e.tensor_reduce(col("mn", t), X[:, t, :], AX.X, ALU.min),
               [xk], ["st_mn"])

    def tiny_ln(quant):
        A = lambda n: stats[:, ST[n], 0:NT]
        ve(lambda e: e.tensor_scalar(A("m"), A("sum"), 1.0 / D, None, ALU.mult),
           ["st_sum"], ["st_m"])
        ve(lambda e: e.tensor_scalar(A("var"), A("sumsq"), 1.0 / D, None, ALU.mult),
           ["st_sumsq"], ["st_var"])
        ve(lambda e: e.tensor_mul(A("t0"), A("m"), A("m")), ["st_m"], ["st_t0"])
        ve(lambda e: e.tensor_sub(A("var"), A("var"), A("t0")),
           ["st_var", "st_t0"], ["st_var"])
        se(lambda e: e.activation(A("t1"), A("var"), ACT.Sqrt, bias=1e-5, scale=1.0),
           ["st_var"], ["st_t1"])
        ve(lambda e: e.reciprocal(A("istd"), A("t1")), ["st_t1"], ["st_istd"])
        if not quant:
            ve(lambda e: e.tensor_copy(A("aa"), A("istd")), ["st_istd"], ["st_aa"])
            ve(lambda e: e.scalar_tensor_tensor(A("bb"), A("m"), -1.0, A("istd"),
                                                ALU.mult, ALU.mult),
               ["st_m", "st_istd"], ["st_bb"])
            return
        ve(lambda e: e.tensor_sub(A("t0"), A("mx"), A("m")),
           ["st_mx", "st_m"], ["st_t0"])
        ve(lambda e: e.tensor_sub(A("t2"), A("m"), A("mn")),
           ["st_mn", "st_m"], ["st_t2"])
        ve(lambda e: e.tensor_max(A("t0"), A("t0"), A("t2")),
           ["st_t0", "st_t2"], ["st_t0"])
        ve(lambda e: e.tensor_mul(A("t0"), A("t0"), A("istd")),
           ["st_t0", "st_istd"], ["st_t0"])
        ve(lambda e: e.tensor_scalar(A("t0"), A("t0"), 1e-5, None, ALU.max),
           ["st_t0"], ["st_t0"])
        ve(lambda e: e.tensor_scalar(A("rs"), A("t0"), 1.0 / 127.0, None, ALU.mult),
           ["st_t0"], ["st_rs"])
        ve(lambda e: e.reciprocal(A("t1"), A("t0")), ["st_t0"], ["st_t1"])
        ve(lambda e: e.scalar_tensor_tensor(A("aa"), A("istd"), 127.0, A("t1"),
                                            ALU.mult, ALU.mult),
           ["st_istd", "st_t1"], ["st_aa"])
        ve(lambda e: e.scalar_tensor_tensor(A("bb"), A("m"), -1.0, A("aa"),
                                            ALU.mult, ALU.mult),
           ["st_m", "st_aa"], ["st_bb"])

    def quant_tile(t, mode):
        qb = t % 2
        if mode == "ln":
            ve(lambda e: e.tensor_scalar(xq_tm[qb][:], X[:, t, :], col("aa", t),
                                         col("bb", t), ALU.mult, ALU.add),
               [f"x{t}", "st_aa", "st_bb"], [f"xq{qb}"])
            return
        nq = len(cur["qscr"])
        q = cur["qscr"][qb % nq]
        qk = f"qscr{qb % nq}"
        ve(lambda e: e.tensor_scalar(q, X[:, t, :], col("aa", t), col("bb", t),
                                     ALU.mult, ALU.add),
           [f"x{t}", "st_aa", "st_bb"], [qk])
        if mode == "int":
            ve(lambda e: e.tensor_scalar(xq_tm[qb][:], q, RC, -RC, ALU.add, ALU.add),
               [qk], [f"xq{qb}"])
        else:
            ve(lambda e: e.tensor_scalar(q, q, RC, -RC, ALU.add, ALU.add), [qk], [qk])
            ve(lambda e: e.tensor_scalar(xq_tm[qb][:], q, col("rs", t), None,
                                         ALU.mult),
               [qk, "st_rs"], [f"xq{qb}"])

    def ffn0():
        mid = [ar.view(f"mid{i}", 4 * i, 4096) for i in range(4)]
        wA = [ar.view(f"wA{i}", 16 + 2 * i, 2048, BF16) for i in range(2)]
        wB = [ar.view(f"wB{i}", 20 + 2 * i, 2048, BF16) for i in range(2)]
        actTb = ar.view("actTg", 24, 2048, BF16)
        mqT = ar.view("mqT", 26, 2048, BF16)
        cur["qscr"] = [ar.view(f"qscr{i}", 28 + i, 1024) for i in range(2)]
        cur["sscr"] = ar.view("sscr", 30, 1024)
        wCu = ar.view2(["wC0", "wC1"], 31, 1024, BF16)
        wCd = [wCu[:, 0:1024].rearrange("p (k n) -> p k n", k=2),
               wCu[:, 1024:2048].rearrange("p (k n) -> p k n", k=2)]
        mq_tm = ar.view("mq_tm", 32, 2048, BF16)
        wAv = [w.rearrange("p (c n) -> p c n", c=8) for w in wA]
        wBv = [w.rearrange("p (c n) -> p c n", c=8) for w in wB]

        for t in range(NT):
            stats_tile(t, True)
        tiny_ln(True)
        ve(lambda e: e.tensor_scalar(cols("gsc", 0, NT), cols("rs", 0, NT),
                                     mcol("gate"), None, ALU.mult),
           ["st_rs", "ms_gate"], ["st_gsc"])
        ve(lambda e: e.tensor_scalar(cols("usc", 0, NT), cols("rs", 0, NT),
                                     mcol("up"), None, ALU.mult),
           ["st_rs", "ms_up"], ["st_usc"])
        for g in range(4):
            tiles = list(range(4 * g, 4 * g + 4))
            cur["qscr"] = [ar.view(f"qscr{i}", 28 + i, 1024) for i in range(2)]
            for t in tiles:
                quant_tile(t, "int")
                qb, j = t % 2, t % 4
                if DBG and t == 0:
                    dma("se", dbg_xq, xq_tm[qb][:], [f"xq{qb}"], ["dbg4"], "dA")
                transpose_pack(xq_tm[qb][:], [f"xq{qb}"],
                               actTb[:, j * 1024:(j + 1) * 1024], f"actT{j}",
                               8, [6, 7], tag="xqT")
                if DBG and t == 0:
                    dma("se", dbg_actT, actTb[:, 0:1024], ["actT0"], ["dbg5"], "dA")
            for n in range(8):
                par = n % 2
                dma("sy", wAv[par].rearrange("p c n -> p (c n)")
                    .rearrange("p (i q) -> p i q", i=4),
                    p_gate[4 * n:4 * n + 4, :, :].transpose([1, 0, 2]),
                    [f"wA{par}", "pd_g"], [f"wA{par}"], "dS", tag="wg")
                dma("sy", wBv[par].rearrange("p c n -> p (c n)")
                    .rearrange("p (i q) -> p i q", i=4),
                    p_up[4 * n:4 * n + 4, :, :].transpose([1, 0, 2]),
                    [f"wB{par}", "pd_u"], [f"wB{par}"], "dS", tag="wu")
                wAr = wA[par].rearrange("p (i c n) -> p i c n", i=4, c=8)
                wBr = wB[par].rearrange("p (i c n) -> p i c n", i=4, c=8)
                for ti, t in enumerate(tiles):
                    j = t % 4
                    aT = actTb[:, j * 1024:(j + 1) * 1024]
                    pr = (4 * n + ti) % 3
                    bg, bu = 2 * pr, 2 * pr + 1
                    mm_group(bg, [(pb[bg][:, :], aT[:, dc * 128:(dc + 1) * 128],
                                   wAr[:, :, dc, :]) for dc in range(8)],
                             [f"actT{j}", f"wA{par}"], tag="mmg")
                    mm_group(bu, [(pb[bu][:, :], aT[:, dc * 128:(dc + 1) * 128],
                                   wBr[:, :, dc, :]) for dc in range(8)],
                             [f"actT{j}", f"wB{par}"], tag="mmu")
                    se(lambda e, bg=bg, t=t: e.activation(
                        cur["sscr"][:, 0:512], pb[bg][:, :], ACT.Silu, bias=0.0,
                        scale=col("gsc", t)),
                       [f"pb{bg}", "st_gsc"], ["sscr"], tag="silu")
                    ve(lambda e, j=j, n=n, bu=bu:
                       e.tensor_mul(mid[j][:, n * 512:(n + 1) * 512],
                                    cur["sscr"][:, 0:512], pb[bu][:, :]),
                       ["sscr", f"pb{bu}"], [f"mid{j}"], tag="midmul")
            for t in tiles:
                j = t % 4
                if DBG and t == 0:
                    dma("se", dbg_mid, mid[0][:], ["mid0"], ["dbg6"], "dA")
                ve(lambda e, j=j, t=t: e.tensor_reduce(col("mmx", t), mid[j][:],
                                                       AX.X, ALU.max,
                                                       apply_absolute_value=True),
                   [f"mid{j}"], ["st_mmx"])
            def tiny_group(lo, hi):
                A2 = lambda n: stats[:, ST[n], lo:hi]
                ve(lambda e: e.tensor_mul(A2("t0"), A2("mmx"), A2("usc")),
                   ["st_mmx", "st_usc"], ["st_t0"])
                ve(lambda e: e.tensor_scalar(A2("t0"), A2("t0"), 1e-5, None, ALU.max),
                   ["st_t0"], ["st_t0"])
                ve(lambda e: e.reciprocal(A2("t1"), A2("t0")), ["st_t0"], ["st_t1"])
                ve(lambda e: e.scalar_tensor_tensor(A2("beta"), A2("usc"), 127.0,
                                                    A2("t1"), ALU.mult, ALU.mult),
                   ["st_usc", "st_t1"], ["st_beta"])
                ve(lambda e: e.tensor_scalar(A2("dsc"), A2("t0"), mcol("down"),
                                             None, ALU.mult),
                   ["st_t0", "ms_down"], ["st_dsc"])
                ve(lambda e: e.tensor_scalar(A2("dsc"), A2("dsc"), 0.5 / 127.0,
                                             None, ALU.mult),
                   ["st_dsc"], ["st_dsc"])
            tiny_group(4 * g, 4 * g + 4)
            if DBG and g == 0:
                dma("se", dbg_st2, stats[:], ["st_beta", "st_dsc", "st_t1", "st_t0",
                                              "st_mmx", "st_usc"], ["dbg7"], "dA")
            mqT2 = ar.view("mqT2", 28, 2048, BF16)
            for sub in range(2):
                subtiles = tiles[2 * sub:2 * sub + 2]
                mqTs = {}
                for si_, t in enumerate(subtiles):
                    j = t % 4
                    se(lambda e, j=j, t=t: e.activation(mid[j][:], mid[j][:],
                                                        ACT.Identity, bias=RC,
                                                        scale=col("beta", t)),
                       [f"mid{j}", "st_beta"], [f"mid{j}"], tag="mq1")
                    ve(lambda e, j=j: e.tensor_scalar(mq_tm[:], mid[j][:], -RC,
                                                      None, ALU.add),
                       [f"mid{j}"], ["mq_tm"], tag="mq2")
                    dstT, dk = (mqT, "mqT") if si_ == 0 else (mqT2, "mqT2")
                    transpose_pack(mq_tm, ["mq_tm"], dstT, dk, 32, [6, 7],
                                   tag="mqT")
                    mqTs[t] = (dstT, dk)
                for h in range(2):
                    for kk in range(0, 32, 2):
                        par = (kk // 2) % 2
                        dma("sy", wCd[par],
                            p_down[kk:kk + 2, :, h * 512:(h + 1) * 512]
                            .transpose([1, 0, 2]),
                            [f"wC{par}", "pd_d"], [f"wC{par}"], "dS", tag="wd")
                        for si_, t in enumerate(subtiles):
                            bd = 2 * si_ + h
                            dstT, dk = mqTs[t]

                            def mmd(e, s, kk=kk, par=par, bd=bd, dstT=dstT):
                                for k in (kk, kk + 1):
                                    inst = e.matmul(
                                        pb[bd][:, :],
                                        dstT[:, k * 128:(k + 1) * 128],
                                        wCd[par][:, k - kk, :],
                                        start=(k == 0), stop=(k == 31))
                                    if k == kk + 1:
                                        inst.then_inc(s["te"], 1)
                            p.step("te", mmd, reads=[dk, f"wC{par}"],
                                   writes=[f"pb{bd}"], tag="mmd")
                    for si_, t in enumerate(subtiles):
                        bd = 2 * si_ + h
                        ve(lambda e, t=t, h=h, bd=bd: e.scalar_tensor_tensor(
                            X[:, t, h * 512:(h + 1) * 512], pb[bd][:, :],
                            col("dsc", t), X[:, t, h * 512:(h + 1) * 512],
                            ALU.mult, ALU.add),
                           [f"pb{bd}", "st_dsc", f"x{t}"], [f"x{t}"], tag="xup")

    def ssm():
        xlnT = ar.view("xlnT", 0, 2048, BF16)
        tfm = ar.view("tfm", 4, 4096, BF16)
        fgfm = ar.view("fgfm", 8, 4096, BF16)
        lam = ar.view("lam", 12, 2048, F32, rows=PR)
        uu = ar.view("uu", 14, 2048, F32, rows=PR)
        As = ar.view("As", 16, 2048, F32, rows=PR)
        Bs = ar.view("Bs", 18, 2048, F32, rows=PR)
        sbfv = ar.view("sbf", 20, 1024, BF16, rows=PR)
        wE = ar.view("wE", 21, 2048, BF16)
        wFf = ar.view("wFf", 23, 2048, BF16)
        wPf = ar.view("wPf", 25, 2048, BF16)
        cur["sscr"] = ar.view("sscr", 27, 1024)
        cw_s = ar.view("cw_s", 28, 512, BF16, rows=PR)
        xlnTv = xlnT.rearrange("p (c n) -> p c n", c=8)
        tfmv = tfm.rearrange("p (c n) -> p c n", c=4)
        fgfmv = fgfm.rearrange("p (c n) -> p c n", c=4)
        wEv = wE.rearrange("p (c n) -> p c n", c=8)
        wFv = wFf.rearrange("p (c n) -> p c n", c=8)
        wPv = wPf.rearrange("p (c n) -> p c n", c=4)
        dma("sy", cw_s[:, 0:1024], p_cw, ["cw_s", "pd_cw"], ["cw_s"], "dS")

        dma("sy", wEv, p_weff.transpose([1, 0, 2]), ["wE", "pd_weff"], ["wE"], "dS")
        dma("sy", wFv, p_wF.transpose([1, 0, 2]), ["wFf", "pd_wF"], ["wFf"], "dS")
        dma("sy", wPv, p_fromP.transpose([1, 0, 2]), ["wPf", "pd_fromP"], ["wPf"], "dS")
        for t in range(NT):
            stats_tile(t, False)
        tiny_ln(False)
        for g in range(4):
            tiles = list(range(4 * g, 4 * g + 4))
            for t in tiles:
                quant_tile(t, "ln")
                qb = t % 2
                bk_ = 6 + (t % 2)

                def tr_fn(e, s, qb=qb, bk_=bk_):
                    for c in range(8):
                        e.transpose(pbf(bk_)[:, c * 128:(c + 1) * 128],
                                    xq_tm[qb][:, c * 128:(c + 1) * 128],
                                    eye_b[:]).then_inc(s["te"], 1)
                p.step("te", tr_fn, reads=[f"xq{qb}", "eye_b"],
                       writes=[f"pb{bk_}"], n_inc=8, tag="xlnT")
                se(lambda e, bk_=bk_, t=t: e.copy(
                    xlnTv[:, :, (t % 4) * 128:(t % 4) * 128 + 128],
                    pbf(bk_)[:, 0:1024].rearrange("p (c n) -> p c n", c=8)),
                   [f"pb{bk_}"], ["xlnTg"], tag="xlnTe")
            gc = slice(g * 512, (g + 1) * 512)
            for pc in range(4):
                bt, bf_ = pc % 2, 2 + pc % 2
                mm_group(bt, [(pb[bt][:, :], wEv[:, dc, pc * 128:(pc + 1) * 128],
                               xlnTv[:, dc, :]) for dc in range(8)],
                         ["wE", "xlnTg"], tag="mmt")
                se(lambda e, pc=pc, bt=bt, gc=gc: e.copy(
                    tfmv[:, pc, gc], pb[bt][:, :]),
                   [f"pb{bt}"], ["tfm"], tag="tfme")
                mm_group(bf_, [(pb[bf_][:, :], wFv[:, dc, pc * 128:(pc + 1) * 128],
                                xlnTv[:, dc, :]) for dc in range(8)],
                         ["wFf", "xlnTg"], tag="mmf")
                se(lambda e, pc=pc, bf_=bf_, gc=gc: e.activation(
                    fgfmv[:, pc, gc], pb[bf_][:, :], ACT.Sigmoid),
                   [f"pb{bf_}"], ["fgfm"], tag="fge")
            mm_group(4, [(pb[4][0:64, :], wlam_s[:, pc, :], fgfmv[:, pc, gc])
                         for pc in range(4)], ["wlam_s", "fgfm"], tag="mml")
            se(lambda e, gc=gc: e.activation(lam[:, gc], pb[4][0:64, :],
                                             ACT.Sigmoid),
               ["pb4"], ["lam"], tag="lame")
            mm_group(5, [(pb[5][0:64, :], bmat_s[:, pc, :], tfmv[:, pc, gc])
                         for pc in range(4)], ["bmat_s", "tfm"], tag="mmu2")
            se(lambda e, gc=gc: e.copy(uu[:, gc], pb[5][0:64, :]),
               ["pb5"], ["uu"], tag="ue")
        ve(lambda e: e.tensor_tensor_scan(As[:], lam[:], lam[:], 1.0,
                                          ALU.mult, ALU.bypass),
           ["lam"], ["As"], tag="scanA")
        ve(lambda e: e.tensor_tensor_scan(Bs[:], lam[:], uu[:], 0.0,
                                          ALU.mult, ALU.add),
           ["lam", "uu"], ["Bs"], tag="scanB")
        ve(lambda e: e.tensor_copy(pubs[:, 0:1], As[:, 2047:2048]),
           ["As"], ["pubs"])
        ve(lambda e: e.tensor_copy(pubs[:, 1:2], Bs[:, 2047:2048]),
           ["pubs", "Bs"], ["pubs"])
        dma("se", cb_scan_in, pubs[:], ["pubs"], ["cbsi"], "dA")
        p.step("ge", lambda e, s: e.collective_compute(
            "AllGather", ALU.bypass, replica_groups=[list(range(NCORES))],
            ins=[cb_scan_in.opt()], outs=[cb_scan_out.opt()]).then_inc(s["cc"], 1),
            reads=["cbsi"], writes=["cbso"], inc="cc", tag="ccscan")
        dma("se", gath[:].rearrange("p (j k) -> p j k", j=8),
            cb_scan_out.rearrange("(j p) k -> p j k", p=PR),
            ["cbso"], ["gath"], "dA")
        ve(lambda e: e.memset(carry[:], 0.0), ["carry"], ["carry"])
        for j in range(NCORES):
            ve(lambda e, j=j: e.tensor_scalar(scAB[:, 0:1], gath[:, 2 * j:2 * j + 1],
                                              masks_s[:, j:j + 1],
                                              masks_s[:, 8 + j:8 + j + 1],
                                              ALU.mult, ALU.add),
               ["gath", "masks_s"], ["scA"])
            ve(lambda e, j=j: e.tensor_scalar(scAB[:, 1:2],
                                              gath[:, 2 * j + 1:2 * j + 2],
                                              masks_s[:, j:j + 1], None, ALU.mult),
               ["gath", "masks_s"], ["scB"])
            ve(lambda e: e.scalar_tensor_tensor(carry[:], carry[:], scAB[:, 0:1],
                                                scAB[:, 1:2], ALU.mult, ALU.add),
               ["carry", "scA", "scB"], ["carry"])
        ve(lambda e: e.scalar_tensor_tensor(As[:], As[:], carry[:, 0:1], Bs[:],
                                            ALU.mult, ALU.add),
           ["As", "Bs", "carry"], ["As"], tag="sfin")
        ve(lambda e: e.tensor_copy(sbfv[:], As[:]), ["As"], ["sbf"], tag="scast")
        for t in range(NT):
            tc = slice(t * 128, (t + 1) * 128)
            for h in range(2):
                bh = h
                pairs = [(pb[bh][:, :], tfmv[:, pc, tc],
                          wPv[:, pc, h * 512:(h + 1) * 512]) for pc in range(4)]
                pairs.append((pb[bh][:, :], sbfv[:, tc], cw_s[0:64, h * 512:(h + 1) * 512]))
                mm_group(bh, pairs, ["tfm", "sbf", "wPf", "cw_s"], tag="mmh")
                ve(lambda e, t=t, h=h, bh=bh: e.scalar_tensor_tensor(
                    X[:, t, h * 512:(h + 1) * 512], pb[bh][:, :], 1.0,
                    X[:, t, h * 512:(h + 1) * 512], ALU.mult, ALU.add),
                   [f"pb{bh}", f"x{t}"], [f"x{t}"], tag="xh")

    def l1ffn(call):
        xqdT = ar.view("xqdT", 0, 8200, BF16)
        xqdTv = xqdT[:, 0:8 * 2050].rearrange("p (c n) -> p c n", c=8)
        h2g = ar.view("h2g", 9, 16384)
        h2gv = h2g.rearrange("p (m n) -> p m n", m=32)
        h2tm = ar.view("h2tm", 25, 4096)
        h2qTu = ar.view2(["h2qT0", "h2qT1", "h2qT2", "h2qT3"], 29, 2048, BF16)
        cur["qscr"] = [ar.view("qscr0", 31, 1024)]
        cur["sscr"] = ar.view("sscr", 32, 1024)
        w1cb = ar.view2(["w1c0", "w1c1"], 33, 1024, BF16)
        w1c = [w1cb[:, 0:1024], w1cb[:, 1024:2048]]
        w1cv = [w.rearrange("p (c n) -> p c n", c=8) for w in w1c]

        for t in range(NT):
            stats_tile(t, True)
        tiny_ln(True)
        for t in range(NT):
            quant_tile(t, "deq")
            qb = t % 2
            bk_ = 6 + (t % 2)

            def tr_fn(e, s, qb=qb, bk_=bk_):
                for c in range(8):
                    e.transpose(pbf(bk_)[:, c * 128:(c + 1) * 128],
                                xq_tm[qb][:, c * 128:(c + 1) * 128],
                                eye_b[:]).then_inc(s["te"], 1)
            p.step("te", tr_fn, reads=[f"xq{qb}", "eye_b"],
                   writes=[f"pb{bk_}"], n_inc=8, tag="xdT")
            se(lambda e, bk_=bk_, t=t: e.copy(
                xqdTv[:, :, 128 * t + 1:128 * t + 129],
                pbf(bk_)[:, 0:1024].rearrange("p (c n) -> p c n", c=8)),
               [f"pb{bk_}"], ["xqdT"], tag="xdTe")
        hp = halo_col[:, 2 * call:2 * call + 2, :]
        ve(lambda e: e.tensor_copy(hp[:, 0, :], xqdTv[:, :, 1]), ["xqdT"],
           ["hpub"], tag="hpub")
        ve(lambda e: e.tensor_copy(hp[:, 1, :], xqdTv[:, :, 2048]),
           ["xqdT", "hpub"], ["hpub"], tag="hpub")
        dma("se", cb_halo_in[call], hp.rearrange("p a b -> p (a b)"),
            ["hpub"], [f"cbhi{call}"], "dA")
        p.step("ge", lambda e, s: e.collective_compute(
            "AllGather", ALU.bypass, replica_groups=[list(range(NCORES))],
            ins=[cb_halo_in[call].opt()], outs=[cb_halo_out[call].opt()])
            .then_inc(s["cc"], 1),
            reads=[f"cbhi{call}"], writes=[f"cbho{call}"], inc="cc", tag="cchalo")
        dma("se", halo_g[:], cb_halo_out[call].rearrange("(j p) k -> p j k", p=128),
            [f"cbho{call}", "halo_g"], ["halo_g"], "dA")
        ve(lambda e: e.memset(xqdTv[:, :, 0], 0.0), ["xqdT"], ["xqdT"])
        ve(lambda e: e.memset(xqdTv[:, :, 2049], 0.0), ["xqdT"], ["xqdT"])
        for j in range(NCORES):
            ve(lambda e, j=j: e.scalar_tensor_tensor(
                xqdTv[:, :, 0], halo_g[:, j, 8:16], masks_h[:, j:j + 1],
                xqdTv[:, :, 0], ALU.mult, ALU.add),
               ["halo_g", "masks_h", "xqdT"], ["xqdT"], tag="hselL")
            ve(lambda e, j=j: e.scalar_tensor_tensor(
                xqdTv[:, :, 2049], halo_g[:, j, 0:8], masks_h[:, 8 + j:8 + j + 1],
                xqdTv[:, :, 2049], ALU.mult, ALU.add),
               ["halo_g", "masks_h", "xqdT"], ["xqdT"], tag="hselR")
        sc_ = cur["sscr"]
        wCu2 = ar.view2(["wC0", "wC1"], 31, 1024, BF16)
        wC2d = [wCu2[:, 0:1024].rearrange("p (k n) -> p k n", k=2),
                wCu2[:, 1024:2048].rearrange("p (k n) -> p k n", k=2)]
        for g in range(4):
            for m in range(32):
                par = m % 2
                dma("sy", w1c[par], p_w1[m, :, :],
                    [f"w1c{par}", "pd_w1"], [f"w1c{par}"], "dS", tag="w1")
                bA = 0 if m % 2 == 0 else 2
                bB = bA + 1
                mm_group(bA, [(pb[bA][:, :], w1cv[par][:, dc, :],
                               xqdTv[:, dc, 512 * g:512 * g + 512])
                              for dc in range(8)],
                         [f"w1c{par}", "xqdT"], tag="mmA")
                mm_group(bB, [(pb[bB][:, 0:2], w1cv[par][:, dc, :],
                               xqdTv[:, dc, 512 * g + 512:512 * g + 514])
                              for dc in range(8)],
                         [f"w1c{par}", "xqdT"], tag="mmB")
                se(lambda e, bA=bA, m=m: e.activation(
                    sc_[:, 0:511], pb[bA][:, 1:512], ACT.Identity,
                    bias=taps[:, m, 3:4], scale=taps[:, m, 1:2]),
                   [f"pb{bA}", "taps_raw", "taps_b"], ["sscr"], tag="cv1")
                se(lambda e, bB=bB, m=m: e.activation(
                    sc_[:, 511:512], pb[bB][:, 0:1], ACT.Identity,
                    bias=taps[:, m, 3:4], scale=taps[:, m, 1:2]),
                   [f"pb{bB}", "taps_raw", "taps_b", "sscr"], ["sscr"], tag="cv1b")
                ve(lambda e, bA=bA, m=m: e.scalar_tensor_tensor(
                    sc_[:, 0:512], pb[bA][:, 0:512], taps[:, m, 0:1],
                    sc_[:, 0:512], ALU.mult, ALU.add),
                   [f"pb{bA}", "taps_raw", "sscr"], ["sscr"], tag="cv2")
                ve(lambda e, bA=bA, m=m: e.scalar_tensor_tensor(
                    sc_[:, 0:510], pb[bA][:, 2:512], taps[:, m, 2:3],
                    sc_[:, 0:510], ALU.mult, ALU.add),
                   [f"pb{bA}", "taps_raw", "sscr"], ["sscr"], tag="cv3")
                ve(lambda e, bB=bB, m=m: e.scalar_tensor_tensor(
                    sc_[:, 510:512], pb[bB][:, 0:2], taps[:, m, 2:3],
                    sc_[:, 510:512], ALU.mult, ALU.add),
                   [f"pb{bB}", "taps_raw", "sscr"], ["sscr"], tag="cv3b")
                se(lambda e, m=m: e.activation(h2gv[:, m, :], sc_[:, 0:512],
                                               ACT.Silu),
                   ["sscr"], ["h2g"], tag="cvsilu")
            for t in range(4 * g, 4 * g + 4):
                tc0 = (t % 4) * 128
                transpose_pack_f(
                    lambda c, tc0=tc0: h2gv[:, c, tc0:tc0 + 128], ["h2g"],
                    lambda pk, nb: h2tm[:, pk * 512:pk * 512 + nb * 128],
                    lambda pk: "h2tm", 32, [4, 5], bf=False, tag="h2tr")
                ve(lambda e, t=t: e.tensor_reduce(col("mmx", t), h2tm[:], AX.X,
                                                  ALU.max,
                                                  apply_absolute_value=True),
                   ["h2tm"], ["st_mmx"], tag="h2amax")
                ve(lambda e, t=t: e.tensor_scalar(col("t0", t), col("mmx", t),
                                                  1e-5, None, ALU.max),
                   ["st_mmx"], ["st_t0"])
                ve(lambda e, t=t: e.reciprocal(col("t1", t), col("t0", t)),
                   ["st_t0"], ["st_t1"])
                ve(lambda e, t=t: e.tensor_scalar(col("beta", t), col("t1", t),
                                                  127.0, None, ALU.mult),
                   ["st_t1"], ["st_beta"])
                ve(lambda e, t=t: e.tensor_scalar(col("dsc", t), col("t0", t),
                                                  mcol("w2"), None, ALU.mult),
                   ["st_t0", "ms_w2"], ["st_dsc"])
                ve(lambda e, t=t: e.tensor_scalar(col("dsc", t), col("dsc", t),
                                                  0.5 / 127.0, None, ALU.mult),
                   ["st_dsc"], ["st_dsc"])
                se(lambda e, t=t: e.activation(h2tm[:], h2tm[:], ACT.Identity,
                                               bias=RC, scale=col("beta", t)),
                   ["h2tm", "st_beta"], ["h2tm"], tag="h2q1")
                for q in range(4):
                    ve(lambda e, q=q: e.tensor_scalar(
                        h2qTu[:, q * 1024:(q + 1) * 1024],
                        h2tm[:, q * 1024:(q + 1) * 1024], -RC, None, ALU.add),
                       ["h2tm"], [f"h2qT{q}"], tag="h2q2")
                for q in range(4):
                    transpose_pack_f(
                        lambda c, q=q: h2qTu[:, q * 1024 + c * 128:
                                             q * 1024 + (c + 1) * 128],
                        [f"h2qT{q}"],
                        lambda pk, nb, q=q: h2qTu[:, q * 1024:q * 1024 + nb * 128],
                        lambda pk, q=q: f"h2qT{q}", 8, [4 + q % 2], bf=True,
                        tag="h2qtr")
                for h in range(2):
                    bd = 6 + h
                    for kk in range(0, 32, 2):
                        par = (kk // 2) % 2
                        dma("sy", wC2d[par],
                            p_w2[kk:kk + 2, :, h * 512:(h + 1) * 512]
                            .transpose([1, 0, 2]),
                            [f"wC{par}", "pd_w2"], [f"wC{par}"], "dS", tag="w2")

                        def mmw2(e, s, kk=kk, par=par, bd=bd):
                            for k in (kk, kk + 1):
                                inst = e.matmul(
                                    pb[bd][:, :],
                                    h2qTu[:, k * 128:(k + 1) * 128],
                                    wC2d[par][:, k - kk, :],
                                    start=(k == 0), stop=(k == 31))
                                if k == kk + 1:
                                    inst.then_inc(s["te"], 1)
                        p.step("te", mmw2,
                               reads=[f"h2qT{kk // 8}", f"h2qT{(kk + 1) // 8}",
                                      f"wC{par}"],
                               writes=[f"pb{bd}"], tag="mmw2")
                    ve(lambda e, t=t, h=h, bd=bd: e.scalar_tensor_tensor(
                        X[:, t, h * 512:(h + 1) * 512], pb[bd][:, :],
                        col("dsc", t), X[:, t, h * 512:(h + 1) * 512],
                        ALU.mult, ALU.add),
                       [f"pb{bd}", "st_dsc", f"x{t}"], [f"x{t}"], tag="xw2")

    def attn():
        qkT = ar.view("qkT", 0, 8192, BF16)
        qkTv = qkT.rearrange("p (t n) -> p t n", t=NT)
        xlnTb = ar.view2(["xlnTa0", "xlnTa1"], 8, 1024, BF16)
        xlnTa = [xlnTb[:, 0:1024], xlnTb[:, 1024:2048]]
        wq3 = {nm: ar.view(f"wq3_{nm}", 9 + 4 * i, 4096, BF16)
               for i, nm in enumerate("qkv")}
        wq3v = {nm: w.rearrange("p (c n) -> p c n", c=8) for nm, w in wq3.items()}
        ab = ar.view("ab", 21, 2048, BF16, rows=1)
        slotb_ = ar.view("slot0", 23, 1040, F32, rows=PR)
        slotb = [slotb_, slotb_]
        kvloc = ar.view("kvloc", 25, 1040, F32, rows=PR)
        kvbf = ar.view("kvbf", 32, 520, BF16)
        kvpair = kvbf[:, 0:8 * 130].rearrange("p (c n) -> p c n", c=8)
        vaugb = ar.view2(["vaug0", "vaug1"], 27, 1040, BF16)
        vaug = [vaugb[:, 0:1040], vaugb[:, 1040:2080]]
        vaugv = [w.rearrange("p (h n) -> p h n", h=NH) for w in vaug]
        kkernb = ar.view2(["kkern0", "kkern1"], 29, 1024, BF16)
        kkern = [kkernb[:, 0:1024], kkernb[:, 1024:2048]]
        ydivb = ar.view2(["ydiv0", "ydiv1"], 30, 1024, BF16)
        ydiv = [ydivb[:, 0:1024], ydivb[:, 1024:2048]]
        cur["sscr"] = ar.view("sscr", 31, 1024)
        for t in range(NT):
            stats_tile(t, False)
        tiny_ln(False)
        for i, nm in enumerate("qkvo"):
            dma("ge", ab[0:1, i * 1024:(i + 1) * 1024], a_b[nm], ["ab"], ["ab"], "dG")
        for nm in "qkv":
            dma("sy", wq3v[nm], p_attn[nm].transpose([1, 0, 2]),
                [f"wq3_{nm}", f"pd_a{nm}"], [f"wq3_{nm}"], "dS")
        ve(lambda e: e.memset(vaugv[0][:, :, 64], 1.0), ["vaug0"], ["vaug0"])
        ve(lambda e: e.memset(vaugv[1][:, :, 64], 1.0), ["vaug1"], ["vaug1"])
        kvoff = [(4, 0, 7), (5, 7, 14), (6, 14, 16)]
        pairoff = [(4, 0, 3), (5, 3, 6), (6, 6, 8)]
        for t in range(NT):
            qb = t % 2
            quant_tile(t, "ln")
            bk_ = 3 if t % 2 == 0 else 7

            def trx(e, s, qb=qb, bk_=bk_):
                for c in range(8):
                    e.transpose(pbf(bk_)[:, c * 128:(c + 1) * 128],
                                xq_tm[qb][:, c * 128:(c + 1) * 128],
                                eye_b[:]).then_inc(s["te"], 1)
            p.step("te", trx, reads=[f"xq{qb}", "eye_b"], writes=[f"pb{bk_}"],
                   n_inc=8, tag="xlnA")
            se(lambda e, bk_=bk_, qb=qb: e.copy(xlnTa[qb][:], pbf(bk_)[:, 0:1024]),
               [f"pb{bk_}"], [f"xlnTa{qb}"], tag="xlnAe")
            for ip, nm in enumerate("qkv"):
                b0, b1 = (0, 1) if (t + ip) % 2 == 0 else (2, 3)
                for h in range(2):
                    bn = b0 if h == 0 else b1
                    pairs = [(pb[bn][:, :], xlnTa[qb][:, dc * 128:(dc + 1) * 128],
                              wq3v[nm][:, dc, h * 512:(h + 1) * 512])
                             for dc in range(8)]
                    pairs.append((pb[bn][:, :], ones_b[:, :],
                                  ab[0:1, ip * 1024 + h * 512:
                                     ip * 1024 + (h + 1) * 512]))
                    mm_group(bn, pairs, [f"xlnTa{qb}", f"wq3_{nm}", "ab", "ones_b"],
                             tag=f"mm{nm}")
                if nm in ("q", "k"):
                    for h in range(2):
                        bn = b0 if h == 0 else b1
                        ve(lambda e, bn=bn, h=h: e.reduce_max(
                            dn_s[:, 8 * h:8 * h + 8],
                            pb[bn][:].rearrange("p (hh e) -> p hh e", hh=8),
                            AX.X, negate=True),
                           [f"pb{bn}"], ["dn_s"], tag="qmax")
                        ve(lambda e, bn=bn, h=h: e.tensor_tensor(
                            cur["sscr"][:, 512 * h:512 * h + 512]
                            .rearrange("p (hh e) -> p hh e", hh=8),
                            pb[bn][:].rearrange("p (hh e) -> p hh e", hh=8),
                            dn_s[:, 8 * h:8 * h + 8].unsqueeze(2)
                            .broadcast_to([128, 8, 64]), ALU.add),
                           [f"pb{bn}", "dn_s", "sscr"], ["sscr"], tag="qsub")
                    if nm == "q":
                        se(lambda e, qb=qb: e.activation(xq_tm[qb][:],
                                                         cur["sscr"][:], ACT.Exp),
                           ["sscr"], [f"xq{qb}"], tag="qexp")
                        bk2 = 3 if t % 2 == 1 else 7

                        def trq(e, s, qb=qb, bk2=bk2, t=t):
                            for c in range(8):
                                e.transpose(pbf(bk2)[:, c * 128:(c + 1) * 128],
                                            xq_tm[qb][:, c * 128:(c + 1) * 128],
                                            eye_b[:]).then_inc(s["te"], 1)
                        p.step("te", trq, reads=[f"xq{qb}", "eye_b"],
                               writes=[f"pb{bk2}"], n_inc=8, tag="qkT")
                        se(lambda e, bk2=bk2, t=t: e.copy(qkTv[:, t, :],
                                                          pbf(bk2)[:, 0:1024]),
                           [f"pb{bk2}"], ["qkT"], tag="qkTe")
                    else:
                        se(lambda e, qb=qb: e.activation(kkern[qb][:],
                                                         cur["sscr"][:], ACT.Exp),
                           ["sscr"], [f"kkern{qb}"], tag="kexp")
                else:
                    for h in range(2):
                        bn = b0 if h == 0 else b1
                        ve(lambda e, bn=bn, h=h, qb=qb: e.tensor_copy(
                            vaugv[qb][:, 8 * h:8 * h + 8, 0:64],
                            pb[bn][:].rearrange("p (hh e) -> p hh e", hh=8)),
                           [f"pb{bn}", f"vaug{qb}"], [f"vaug{qb}"], tag="vev")
            def kvacc(e, s, t=t, qb=qb):
                last = None
                for bank, h0, h1 in kvoff:
                    for h in range(h0, h1):
                        off = (h - h0) * 65
                        inst = e.matmul(pb[bank][0:64, off:off + 65],
                                        kkern[qb][:, h * 64:(h + 1) * 64],
                                        vaugv[qb][:, h, :],
                                        start=(t == 0 and h == h0),
                                        stop=(t == NT - 1 and h == h1 - 1))
                        last = inst
                        if t == NT - 1:
                            inst.then_inc(s["te"], 1)
                if t != NT - 1:
                    last.then_inc(s["te"], 1)
            p.step("te", kvacc, reads=[f"kkern{qb}", f"vaug{qb}"],
                   writes=["pb4", "pb5", "pb6"],
                   n_inc=(16 if t == NT - 1 else 1), tag="kvacc")
        if ATTN_STAGE < 2:
            return
        for bank, h0, h1 in kvoff:
            ve(lambda e, bank=bank, h0=h0, h1=h1: e.tensor_copy(
                kvloc[:, h0 * 65:h1 * 65], pb[bank][0:64, 0:(h1 - h0) * 65]),
               [f"pb{bank}"], ["kvloc"], tag="kvev")
        dma("se", cb_kv_in, kvloc[:], ["kvloc"], ["cbkvi"], "dA")
        p.step("ge", lambda e, s: e.collective_compute(
            "AllGather", ALU.bypass, replica_groups=[list(range(NCORES))],
            ins=[cb_kv_in.opt()], outs=[cb_kv_out.opt()]).then_inc(s["cc"], 1),
            reads=["cbkvi"], writes=["cbkvo"], inc="cc", tag="cckv")
        ve(lambda e: e.memset(kvloc[:], 0.0), ["kvloc", "cbkvi"], ["kvloc"])
        for j in range(NCORES):
            sb = slotb[j % 2]
            sk = "slot0"
            dma("se", sb[:], cb_kv_out[j * PR:(j + 1) * PR, :], ["cbkvo", sk],
                [sk], "dA")
            ve(lambda e, sb=sb, j=j: e.scalar_tensor_tensor(
                kvloc[:], sb[:], masks_kv[:, j:j + 1], kvloc[:],
                ALU.mult, ALU.add),
               [sk, "masks_kv", "kvloc"], ["kvloc"], tag="kvfold")
        ve(lambda e: e.memset(kvbf[:], 0.0), ["kvbf"], ["kvbf"], tag="kvz")
        kvlv = kvloc[:, 0:NH * 65].rearrange("p (c k n) -> p c k n", c=8, k=2)
        ve(lambda e: e.tensor_copy(kvpair[0:64, :, 0:65], kvlv[:, :, 0, :]),
           ["kvloc", "kvbf"], ["kvbf"], tag="kvcast")
        dma("ge", kvpair[64:128, :, 65:130], kvlv[:, :, 1, :],
            ["kvloc", "kvbf"], ["kvbf"], "dG", tag="kvdup", slow=True)
        if ATTN_STAGE < 3:
            return
        woT = ar.view("woT", 9, 4096, BF16)
        woTv = woT.rearrange("p (c n) -> p c n", c=8)
        dma("sy", woTv, p_attn["o"].transpose([1, 0, 2]), ["woT", "pd_ao"],
            ["woT"], "dS")
        for t in range(NT):
            qb = t % 2

            def ymm(e, s, t=t):
                for bank, c0, c1 in pairoff:
                    for c in range(c0, c1):
                        off = (c - c0) * 130
                        inst = e.matmul(
                            pb[bank][:, off:off + 130],
                            qkTv[:, t, c * 128:(c + 1) * 128],
                            kvpair[:, c, :],
                            start=(c == c0), stop=(c == c1 - 1))
                        inst.then_inc(s["te"], 1)
            p.step("te", ymm, reads=["qkT", "kvbf"],
                   writes=["pb4", "pb5", "pb6"], n_inc=8, tag="ymm")
            if ATTN_STAGE < 4:
                continue
            for bank, c0, c1 in pairoff:
                nc_ = c1 - c0
                ve(lambda e, bank=bank, c0=c0, nc_=nc_: e.tensor_scalar(
                    dn_s[:, 2 * c0:2 * (c0 + nc_)]
                    .rearrange("p (c k) -> p c k", k=2),
                    pb[bank][:, 0:nc_ * 130]
                    .rearrange("p (c k n) -> p c k n", c=nc_, k=2)[:, :, :, 64],
                    1e-6, None, ALU.add),
                   [f"pb{bank}"], ["dn_s"], tag="dnx")
            ve(lambda e: e.reciprocal(rd_s[:, :], dn_s[:, :]), ["dn_s"], ["rd_s"],
               tag="rdx")
            for bank, c0, c1 in pairoff:
                nc_ = c1 - c0
                ve(lambda e, bank=bank, c0=c0, nc_=nc_, qb=qb: e.tensor_tensor(
                    ydiv[qb][:, c0 * 128:(c0 + nc_) * 128]
                    .rearrange("p (c k e) -> p c k e", c=nc_, k=2),
                    pb[bank][:, 0:nc_ * 130]
                    .rearrange("p (c k n) -> p c k n", c=nc_, k=2)[:, :, :, 0:64],
                    rd_s[:, 2 * c0:2 * (c0 + nc_)]
                    .rearrange("p (c k) -> p c k", k=2).unsqueeze(3)
                    .broadcast_to([128, nc_, 2, 64]),
                    ALU.mult),
                   [f"pb{bank}", "rd_s", f"ydiv{qb}"], [f"ydiv{qb}"], tag="ydv")
            if ATTN_STAGE < 5:
                continue
            bk2 = 3 if t % 2 == 0 else 7

            def try_(e, s, qb=qb, bk2=bk2):
                for c in range(8):
                    e.transpose(pbf(bk2)[:, c * 128:(c + 1) * 128],
                                ydiv[qb][:, c * 128:(c + 1) * 128],
                                eye_b[:]).then_inc(s["te"], 1)
            p.step("te", try_, reads=[f"ydiv{qb}", "eye_b"], writes=[f"pb{bk2}"],
                   n_inc=8, tag="ydT")
            se(lambda e, bk2=bk2, qb=qb: e.copy(xq_tm[qb][:], pbf(bk2)[:, 0:1024]),
               [f"pb{bk2}"], [f"xq{qb}"], tag="ydTe")
            if ATTN_STAGE < 6:
                continue
            for h in range(2):
                bn = h + 2 * (t % 2)
                pairs = [(pb[bn][:, :], xq_tm[qb][:, dc * 128:(dc + 1) * 128],
                          woTv[:, dc, h * 512:(h + 1) * 512]) for dc in range(8)]
                pairs.append((pb[bn][:, :], ones_b[:, :],
                              ab[0:1, 3 * 1024 + h * 512:3 * 1024 + (h + 1) * 512]))
                mm_group(bn, pairs, [f"xq{qb}", "woT", "ab", "ones_b"], tag="mmo")
                ve(lambda e, t=t, h=h, bn=bn: e.scalar_tensor_tensor(
                    X[:, t, h * 512:(h + 1) * 512], pb[bn][:, :], 1.0,
                    X[:, t, h * 512:(h + 1) * 512], ALU.mult, ALU.add),
                   [f"pb{bn}", f"x{t}"], [f"x{t}"], tag="xattn")

    def store_output():
        dma("se", out_t.rearrange("(t p) d -> p t d", p=128), X[:],
            [f"x{t}" for t in range(NT)], ["outb"], "dA")
        p.step("se", lambda e, s: e.wait_ge(s["dA"], p.count["dA"]),
               reads=["outb"], writes=[], n_inc=0)
        p.step("ge", lambda e, s: e.wait_ge(s["dA"], p.count["dA"]),
               reads=["outb"], writes=[], n_inc=0)

    prep_quant_resident(w_gate, p_gate, "gate", INNER, D, "g")
    prep_quant_resident(w_up, p_up, "up", INNER, D, "u")
    prep_quant_resident(w_down, p_down, "down", D, INNER, "d")
    if 2 in SUBLAYERS:
        prep_ssm()
    if 4 in SUBLAYERS or 6 in SUBLAYERS:
        prep_quant_resident(w1_d, p_w1, "w1", INNER, D, "w1")
        prep_quant_resident(w2_d, p_w2, "w2", D, INNER, "w2")
        prep_taps()
    if 5 in SUBLAYERS:
        for nm in "qkvo":
            prep_attn_w(a_w[nm], p_attn[nm], f"a{nm}")

    if DBG:
        dma("se", dbg_pg, p_gate[:, :, 0:512], ["pd_g"], ["dbg1"], "dA")
    if 1 in SUBLAYERS:
        ffn0()
    if DBG:
        dma("se", dbg_msw, msw[:], ["ms_gate", "ms_up", "ms_down"], ["dbg2"], "dA")
        dma("se", dbg_stats, stats[:], ["st_aa", "st_bb", "st_rs", "st_mmx",
                                        "st_beta", "st_dsc"], ["dbg3"], "dA")
    if 2 in SUBLAYERS:
        ssm()
    if 3 in SUBLAYERS:
        ffn0()
    if 4 in SUBLAYERS:
        l1ffn(0)
    if 5 in SUBLAYERS:
        attn()
    if 6 in SUBLAYERS:
        l1ffn(1)
    store_output()

    p.emit(sems)
    nc.finalize()
    ctx.close()
    return nc


_CACHE = {}


def _get_nc():
    if "nc" not in _CACHE:
        _CACHE["nc"] = build_program()
    return _CACHE["nc"]


def _shard_inputs(inputs):
    f32 = np.float32
    x = np.asarray(inputs["L0_attn_x"], f32)
    common = {
        "w_toP": np.ascontiguousarray(inputs["L0_attn_W_toP"], dtype=f32),
        "w_U": np.ascontiguousarray(np.asarray(inputs["L0_attn_W_U"], f32).reshape(D, P5 * 4)),
        "w_F": np.ascontiguousarray(inputs["L0_attn_W_F"], dtype=f32),
        "w_V": np.ascontiguousarray(np.asarray(inputs["L0_attn_W_V"], f32).reshape(1, P5 * 4)),
        "w_lam": np.ascontiguousarray(inputs["L0_attn_W_lam"], dtype=f32),
        "b_mat": np.ascontiguousarray(inputs["L0_attn_B_mat"], dtype=f32),
        "c_mat": np.ascontiguousarray(inputs["L0_attn_C_mat"], dtype=f32),
        "w_fromP": np.ascontiguousarray(inputs["L0_attn_W_fromP"], dtype=f32),
        "w_gate": np.ascontiguousarray(inputs["L0_ffn_w_gate"], dtype=f32),
        "w_up": np.ascontiguousarray(inputs["L0_ffn_w_up"], dtype=f32),
        "w_down": np.ascontiguousarray(inputs["L0_ffn_w_down"], dtype=f32),
        "w1": np.ascontiguousarray(inputs["L1_ffn_w1"], dtype=f32),
        "w2": np.ascontiguousarray(inputs["L1_ffn_w2"], dtype=f32),
        "conv_w": np.ascontiguousarray(np.asarray(inputs["L1_ffn_conv_w"], f32).reshape(INNER, 3)),
        "conv_b": np.ascontiguousarray(np.asarray(inputs["L1_ffn_conv_b"], f32).reshape(1, INNER)),
        "eye_bf": np.eye(128, dtype=BFNP),
        "eye_f32": np.eye(128, dtype=f32),
        "cpool_d": np.tile(np.array([[RC, 1e-5, 0.0, 1.0]], f32), (128, 1)),
    }
    for nm in "qkvo":
        common[f"W{nm}"] = np.ascontiguousarray(inputs[f"L1_attn_W{nm}"], dtype=f32)
        common[f"b{nm}"] = np.ascontiguousarray(
            np.asarray(inputs[f"L1_attn_b{nm}"], f32).reshape(1, D))
    maps = []
    for c in range(NCORES):
        b, t0 = c // 4, (c % 4) * TOK
        m = dict(common)
        m["x_in"] = np.ascontiguousarray(x[b, t0:t0 + TOK, :])
        sm = np.zeros((PR, 16), f32)
        km = np.zeros((PR, 8), f32)
        hm = np.zeros((128, 16), f32)
        for j in range(NCORES):
            same = (j // 4) == (c // 4)
            sm[:, j] = 1.0 if (same and j < c) else 0.0
            sm[:, 8 + j] = 1.0 - sm[0, j]
            km[:, j] = 1.0 if same else 0.0
            hm[:, j] = 1.0 if (same and j == c - 1) else 0.0
            hm[:, 8 + j] = 1.0 if (same and j == c + 1) else 0.0
        m["scanmask"], m["kvmask"], m["halomask"] = sm, km, hm
        maps.append(m)
    return maps


def kernel(**inputs):
    nc = _get_nc()
    maps = _shard_inputs(inputs)
    res = run_bass_kernel_spmd(nc, maps, core_ids=list(range(NCORES)))
    out = np.empty((B, T, D), np.float32)
    for c in range(NCORES):
        b, t0 = c // 4, (c % 4) * TOK
        out[b, t0:t0 + TOK, :] = res.results[c]["out"]
    return out
